# version 09 (final)
# baseline (speedup 1.0000x reference)
import numpy as np
from contextlib import ExitStack

import concourse.bass as bass
import concourse.mybir as mybir
from concourse.bass_utils import run_bass_kernel_spmd

F32 = mybir.dt.float32
BF16 = mybir.dt.bfloat16
AF = mybir.ActivationFunctionType
ALU = mybir.AluOpType
AX = mybir.AxisListType

T = 4096
D = 1024
NTILE = T // 128
NQUAD = T // 512
NCORES = 8
NEG = -1.0e30

ENGS = ("t", "v", "s", "g", "y")


class Ctr:
    __slots__ = ("sem", "n")

    def __init__(self, sem):
        self.sem = sem
        self.n = 0


class Bank:
    __slots__ = ("t", "guard")

    def __init__(self, t):
        self.t = t
        self.guard = None


class Prog:
    def __init__(self, nc, es):
        self.nc = nc
        self.es = es
        self.thunks = {e: [] for e in ENGS}
        self.ctr = {e: Ctr(es.enter_context(nc.semaphore(f"c_{e}"))) for e in ENGS}
        self.dctr = {e: Ctr(es.enter_context(nc.semaphore(f"d_{e}"))) for e in ("g", "y")}
        self.lastw = {}
        self.taps = {}
        self.want_taps = set()
        self.flt = {}
        self.vflush_cov = 0
        self.vflush_tok = None

    def _norm_waits(self, eng, w):
        out = []
        stack = list(w)
        for item in stack:
            if item is None:
                continue
            if isinstance(item, Ctr):
                ctr, val = item, item.n
            elif (isinstance(item, tuple) and len(item) == 2
                  and isinstance(item[0], Ctr)):
                ctr, val = item
            elif isinstance(item, (list, tuple)):
                stack.extend(item)
                continue
            else:
                raise TypeError(f"bad wait token {item!r}")
            if val <= 0:
                continue
            key = (eng, id(ctr))
            if self.lastw.get(key, 0) >= val:
                continue
            self.lastw[key] = val
            out.append((ctr.sem, val))
        return out

    def vflush(self, engines=("v", "s")):
        toks = []
        for e in engines:
            fl = self.flt.get(e)
            if fl is None:
                continue
            meth = "tensor_copy" if e == "v" else "copy"
            toks.append(self.op(e, meth, fl, fl, _noflush=True))
        return toks

    def _flush_v_for(self, val):
        if self.vflush_tok is None or val > self.vflush_cov:
            self.vflush_cov = self.ctr["v"].n
            fl = self.flt.get("v")
            self.vflush_tok = self.op("v", "tensor_copy", fl, fl,
                                      _noflush=True)
        return self.vflush_tok

    def op(self, eng, method, *args, w=(), _noflush=False, **kw):
        if eng != "v" and not _noflush and self.flt:
            vctr = self.ctr["v"]
            w2 = []
            need = 0
            stack = list(w)
            for item in stack:
                if isinstance(item, tuple) and len(item) == 2 and item[0] is vctr:
                    need = max(need, item[1])
                elif item is vctr:
                    need = max(need, vctr.n)
                elif isinstance(item, (list, tuple)) and not (
                        isinstance(item, tuple) and len(item) == 2
                        and isinstance(item[0], Ctr)):
                    stack.extend(item)
            if need > 0:
                w = tuple(w) + (self._flush_v_for(need),)
        waits = self._norm_waits(eng, w)
        if method in ("dma_start", "dma_start_transpose"):
            c = self.dctr[eng]
            k = 16
        else:
            c = self.ctr[eng]
            k = 1
        c.n += k
        val = c.n
        sem = c.sem

        def th(e, _m=method, _a=args, _k=kw, _w=waits, _s=sem, _i=k):
            for s_, v_ in _w:
                e.wait_ge(s_, v_)
            getattr(e, _m)(*_a, **_k).then_inc(_s, _i)

        self.thunks[eng].append(th)
        return (c, val)

    def waitonly(self, eng, *tokens):
        if eng != "v" and self.flt:
            vctr = self.ctr["v"]
            need = 0
            stack = list(tokens)
            for item in stack:
                if isinstance(item, tuple) and len(item) == 2 and item[0] is vctr:
                    need = max(need, item[1])
                elif item is vctr:
                    need = max(need, vctr.n)
                elif isinstance(item, (list, tuple)) and not (
                        isinstance(item, tuple) and len(item) == 2
                        and isinstance(item[0], Ctr)):
                    stack.extend(item)
            if need > 0:
                tokens = tuple(tokens) + (self._flush_v_for(need),)
        waits = self._norm_waits(eng, tokens)
        if not waits:
            return

        def th(e, _w=waits):
            for s_, v_ in _w:
                e.wait_ge(s_, v_)

        self.thunks[eng].append(th)

    def fence(self, eng):
        self.waitonly(eng, self.ctr[eng])

    def tok(self, eng):
        return (self.ctr[eng], self.ctr[eng].n)

    def dtok(self, eng):
        return (self.dctr[eng], self.dctr[eng].n)

    def barrier(self):
        toks = [self.ctr[e] for e in ENGS] + [self.dctr["g"], self.dctr["y"]]
        snap = [(c, c.n) for c in toks]
        for e in ENGS:
            self.waitonly(e, *snap)

    def acquire(self, bank):
        if bank.guard is not None:
            self.waitonly("t", bank.guard)

    def tap(self, name, ap, shape, dtype=F32):
        if name not in self.want_taps:
            return
        d = self.nc.dram_tensor(f"tap_{name}", list(shape), dtype,
                                kind="ExternalOutput").ap()
        self.op("y", "dma_start", out=d, in_=ap,
                w=(self.ctr["v"], self.ctr["s"], self.ctr["t"]))
        self.taps[name] = shape

    def finalize(self):
        with self.nc.Block() as block:
            block.tensor(lambda e: [f(e) for f in self.thunks["t"]])
            block.vector(lambda e: [f(e) for f in self.thunks["v"]])
            block.scalar(lambda e: [f(e) for f in self.thunks["s"]])
            block.gpsimd(lambda e: [f(e) for f in self.thunks["g"]])
            block.sync(lambda e: [f(e) for f in self.thunks["y"]])



def host_constants():
    c = {}
    c["ident_f"] = np.eye(128, dtype=np.float32)
    c["ident_b"] = np.eye(128, dtype=np.float32)
    ii = np.arange(128)[:, None]
    jj = np.arange(512)[None, :]
    m = np.zeros((128, 4, 512), dtype=np.float32)
    for v in range(4):
        m[:, v, :] = np.where(v * 128 + ii <= jj, 0.0, NEG)
    c["cmask"] = m
    c["eps_col"] = np.full((128, 1), 1e-5, dtype=np.float32)
    c["ones_f"] = np.ones((128, 128), dtype=np.float32)
    qq = np.arange(128)[:, None]
    kk = np.arange(128)[None, :]
    wm = np.zeros((128, 2, 128), dtype=np.float32)
    wm[:, 0, :] = np.where(kk >= qq, 0.0, NEG)
    wm[:, 1, :] = np.where(kk <= qq, 0.0, NEG)
    c["wmask"] = wm
    return c



class LayerCommon:

    def __init__(self, p, sb, x_in, tag):
        nc, es = p.nc, p.es
        self.p = p
        self.sb = sb
        self.x_in = x_in
        self.mean_all = es.enter_context(nc.sbuf_tensor(f"{tag}_mean", [128, NTILE], F32))
        self.rstd_all = es.enter_context(nc.sbuf_tensor(f"{tag}_rstd", [128, NTILE], F32))
        self.mv = es.enter_context(nc.sbuf_tensor(f"{tag}_mv", [128, 2], F32))
        self.sd_col = es.enter_context(nc.sbuf_tensor(f"{tag}_sd", [128, 1], F32))
        self.rstd_col = es.enter_context(nc.sbuf_tensor(f"{tag}_rsc", [128, 1], F32))
        self.cols = es.enter_context(nc.sbuf_tensor(f"{tag}_cols", [128, 4], F32))
        self.pb = [Bank(es.enter_context(nc.psum_tensor(f"{tag}_ps{i}", [128, 512], F32)))
                   for i in range(6)]
        self.ptr = es.enter_context(nc.psum_tensor(f"{tag}_ptr", [128, 1024], BF16))
        self.ptr_bank = Bank(self.ptr)
        self.tag = tag
        self.nalloc = 0
        self.xg = [None, None]

    def alloc_tiles(self, scope, nx=2, with_f32=False):
        nc = self.p.nc
        i = self.nalloc
        self.nalloc += 1
        self.nx = nx
        self.with_f32 = with_f32
        if with_f32:
            self.xn_f = scope.enter_context(
                nc.sbuf_tensor(f"{self.tag}_xnf{i}", [128, D], F32))
            self.xnT_f = scope.enter_context(
                nc.sbuf_tensor(f"{self.tag}_xnTf{i}", [128, 8, 128], F32))
        self.x_sb = scope.enter_context(
            nc.sbuf_tensor(f"{self.tag}_x{i}", [128, nx, D], F32))
        self.xn_tok = scope.enter_context(
            nc.sbuf_tensor(f"{self.tag}_xn{i}", [128, D], BF16))
        self.xnT = scope.enter_context(
            nc.sbuf_tensor(f"{self.tag}_xnT{i}", [128, 8, 512], BF16))
        self.sq_scr = scope.enter_context(
            nc.sbuf_tensor(f"{self.tag}_sqs{i}", [128, D], BF16))
        self.xg = [None, None]

    def ln_tile(self, t, xi):
        p = self.p
        xi = xi % self.nx
        tk = p.op("y", "dma_start", out=self.x_sb[:, xi, :],
                  in_=self.x_in[t * 128:(t + 1) * 128, :], w=(self.xg[xi],))
        tks = p.op("s", "activation", self.sq_scr[:, :], self.x_sb[:, xi, :],
                   AF.Square, accum_out=self.cols[:, 1:2], w=(tk, p.ctr["v"]))
        p.op("v", "reduce_sum", self.cols[:, 0:1], self.x_sb[:, xi, :], AX.X,
             w=(tk,))
        p.fence("v")
        p.op("v", "tensor_scalar", self.mv[:, 0:1], self.cols[:, 0:1],
             1.0 / D, None, ALU.mult)
        p.fence("v")
        p.op("v", "tensor_mul", self.cols[:, 2:3], self.mv[:, 0:1], self.mv[:, 0:1])
        p.op("v", "tensor_scalar", self.cols[:, 3:4], self.cols[:, 1:2],
             1.0 / D, None, ALU.mult, w=(tks,))
        p.fence("v")
        p.op("v", "tensor_sub", self.cols[:, 3:4], self.cols[:, 3:4],
             self.cols[:, 2:3])
        p.fence("v")
        p.op("s", "activation", self.sd_col[:, :], self.cols[:, 3:4], AF.Sqrt,
             bias=self.sb["eps_col"][:, :], w=(p.ctr["v"],))
        p.op("v", "reciprocal", self.rstd_col[:, :], self.sd_col[:, :],
             w=(p.ctr["s"],))
        p.fence("v")
        p.op("v", "tensor_copy", self.mean_all[:, t:t + 1], self.mv[:, 0:1])
        p.op("v", "tensor_copy", self.rstd_all[:, t:t + 1], self.rstd_col[:, :])
        if self.with_f32:
            tk = p.op("v", "tensor_scalar", self.xn_f[:, :], self.x_sb[:, xi, :],
                      self.mv[:, 0:1], self.rstd_col[:, :],
                      ALU.subtract, ALU.mult, w=(p.ctr["t"], p.ctr["s"]))
            tk = p.op("s", "copy", self.xn_tok[:, :], self.xn_f[:, :], w=(tk,))
        else:
            tk = p.op("v", "tensor_scalar", self.xn_tok[:, :],
                      self.x_sb[:, xi, :], self.mv[:, 0:1], self.rstd_col[:, :],
                      ALU.subtract, ALU.mult, w=(p.ctr["t"],))
        self.xg[xi] = tk
        return tk

    def xn_tile_from_stats(self, t, xi):
        p = self.p
        xi = xi % self.nx
        tk = p.op("y", "dma_start", out=self.x_sb[:, xi, :],
                  in_=self.x_in[t * 128:(t + 1) * 128, :], w=(self.xg[xi],))
        if self.with_f32:
            tk = p.op("v", "tensor_scalar", self.xn_f[:, :], self.x_sb[:, xi, :],
                      self.mean_all[:, t:t + 1], self.rstd_all[:, t:t + 1],
                      ALU.subtract, ALU.mult, w=(tk, p.ctr["t"], p.ctr["s"]))
            tk = p.op("s", "copy", self.xn_tok[:, :], self.xn_f[:, :], w=(tk,))
        else:
            tk = p.op("v", "tensor_scalar", self.xn_tok[:, :],
                      self.x_sb[:, xi, :], self.mean_all[:, t:t + 1],
                      self.rstd_all[:, t:t + 1],
                      ALU.subtract, ALU.mult, w=(tk, p.ctr["t"]))
        self.xg[xi] = tk
        return tk

    def transpose_xnf(self, banks):
        p = self.p
        for half in range(2):
            bk = banks[half]
            p.acquire(bk)
            for j in range(4):
                dc = half * 4 + j
                p.op("t", "transpose", bk.t[:, j * 128:(j + 1) * 128],
                     self.xn_f[:, dc * 128:(dc + 1) * 128],
                     self.sb["ident_f"][:, :], w=(p.ctr["s"], p.ctr["v"]))
            bk.guard = p.op("v", "tensor_copy",
                            self.xnT_f[:, half * 4:half * 4 + 4, :],
                            bk.t[:, :].rearrange("p (c m) -> p c m", c=4),
                            w=(p.ctr["t"],))
        return p.tok("v")

    def transpose_xn(self, s):
        p = self.p
        p.acquire(self.ptr_bank)
        for dc in range(8):
            p.op("t", "transpose", self.ptr[:, dc * 128:(dc + 1) * 128],
                 self.xn_tok[:, dc * 128:(dc + 1) * 128], self.sb["ident_b"][:, :],
                 w=(p.ctr["v"], p.ctr["s"]))
        tk = p.op("v", "tensor_copy",
                  self.xnT[:, :, s * 128:(s + 1) * 128],
                  self.ptr[:, :].rearrange("p (c m) -> p c m", c=8),
                  w=(p.ctr["t"],))
        self.ptr_bank.guard = tk
        return tk




def emit_mla_layer(p, io, cfg, ffn_fn, dbg=False):
    nc, es = p.nc, p.es
    sb = io["sb"]
    ident_f = sb["ident_f"]
    ones_b = sb["ones_b"]
    cmask = sb["cmask"]

    tag, wp = cfg["tag"], cfg["wp"]
    QL, KVL = cfg["QL"], cfg["KVL"]
    QLC, LC = QL // 128, KVL // 128
    NH = 16
    x_in, x_out, y_attn_d = io["x_in"], io["x_out"], io["y_attn_d"]
    W = io["w"]

    lc_ = LayerCommon(p, sb, x_in, tag)
    pb = lc_.pb

    wuq = es.enter_context(nc.sbuf_tensor(f"{tag}_wuq", [128, 8, QL], BF16))
    keff = es.enter_context(nc.sbuf_tensor(f"{tag}_keff", [128, 8, KVL], BF16))
    sA = es.enter_context(ExitStack())
    wdqT = sA.enter_context(nc.sbuf_tensor(f"{tag}_wdqT", [128, 8, QL], BF16))
    wdkvT = sA.enter_context(nc.sbuf_tensor(f"{tag}_wdkvT", [128, 8, KVL], BF16))
    veff = sA.enter_context(nc.sbuf_tensor(f"{tag}_veff", [128, LC, 1024], BF16))
    lc_.alloc_tiles(sA)
    sW = ExitStack()
    wdq = sW.enter_context(nc.sbuf_tensor(f"{tag}_wdq", [128, QLC, 1024], BF16))
    wuk = sW.enter_context(nc.sbuf_tensor(f"{tag}_wuk", [128, 8, KVL], BF16))
    wuv = sW.enter_context(nc.sbuf_tensor(f"{tag}_wuv", [128, 8, KVL], BF16))
    m1sb = sW.enter_context(nc.sbuf_tensor(f"{tag}_m1", [128, QLC, KVL], BF16))
    woT = sW.enter_context(nc.sbuf_tensor(f"{tag}_woT", [128, 8, 1024], BF16))

    wdq_d = nc.dram_tensor(f"{tag}_wdq_d", [QL, 1024], BF16, kind="Internal").ap()
    wdkv_d = nc.dram_tensor(f"{tag}_wdkv_d", [KVL, 1024], BF16, kind="Internal").ap()
    wo_d = nc.dram_tensor(f"{tag}_wo_d", [1024, 1024], BF16, kind="Internal").ap()

    g = lambda out, in_, **kw: p.op("g", "dma_start", out=out, in_=in_, **kw)
    tok_wuq = g(wuq[:, :, :], W[wp + "_W_uq"].rearrange("(c p) m -> p c m", p=128))
    tok_wuk = g(wuk[:, :, :], W[wp + "_W_uk"].rearrange("(c p) m -> p c m", p=128))
    tok_wuv = g(wuv[:, :, :], W[wp + "_W_uv"].rearrange("(c p) m -> p c m", p=128))
    tok_wdq = g(wdq[:, :, :], W[wp + "_W_dq"].rearrange("(a p) c -> p a c", p=128))
    tok_dq_d = g(wdq_d, W[wp + "_W_dq"])
    tok_dkv_d = g(wdkv_d, W[wp + "_W_dkv"])
    tok_wo_d = g(wo_d, W[wp + "_W_o"])

    for cc in range(8):
        p.op("y", "dma_start", out=wdqT[:, cc, :],
             in_=wdq_d[:, cc * 128:(cc + 1) * 128], transpose=True, w=(tok_dq_d,))
        p.op("y", "dma_start", out=wdkvT[:, cc, :],
             in_=wdkv_d[:, cc * 128:(cc + 1) * 128], transpose=True, w=(tok_dkv_d,))
        p.op("y", "dma_start", out=woT[:, cc, :],
             in_=wo_d[:, cc * 128:(cc + 1) * 128], transpose=True, w=(tok_wo_d,))
    tok_ydma = p.dtok("y")

    for qlc in range(QLC):
        bk = pb[qlc % 4]
        p.acquire(bk)
        for cc in range(8):
            p.op("t", "matmul", bk.t[:, 0:KVL], wuq[:, cc, qlc * 128:(qlc + 1) * 128],
                 wuk[:, cc, :], start=(cc == 0), stop=(cc == 7),
                 w=(tok_wuq, tok_wuk))
        bk.guard = p.op("s", "copy", m1sb[:, qlc, :], bk.t[:, 0:KVL], w=(p.ctr["t"],))
    for cc in range(8):
        bk = pb[cc % 4]
        p.acquire(bk)
        for qlc in range(QLC):
            p.op("t", "matmul", bk.t[:, 0:KVL], wdq[:, qlc, cc * 128:(cc + 1) * 128],
                 m1sb[:, qlc, :], start=(qlc == 0), stop=(qlc == QLC - 1),
                 w=(tok_wdq, p.ctr["s"]))
        bk.guard = p.op("s", "copy", keff[:, cc, :], bk.t[:, 0:KVL], w=(p.ctr["t"],))
    for lb in range(LC):
        for grp in range(2):
            bk = pb[(lb * 2 + grp) % 4]
            p.acquire(bk)
            for cc in range(8):
                p.op("t", "matmul", bk.t[:, :], wuv[:, cc, lb * 128:(lb + 1) * 128],
                     woT[:, cc, grp * 512:(grp + 1) * 512],
                     start=(cc == 0), stop=(cc == 7), w=(tok_wuv, tok_ydma))
            bk.guard = p.op("s", "copy", veff[:, lb, grp * 512:(grp + 1) * 512],
                            bk.t[:, :], w=(p.ctr["t"],))

    p.barrier()
    sW.close()

    sAB = es.enter_context(ExitStack())
    t1T = sAB.enter_context(nc.sbuf_tensor(f"{tag}_t1T", [128, QLC, T], BF16,
                                           side="right"))
    ckvT = sAB.enter_context(nc.sbuf_tensor(f"{tag}_ckvT", [128, LC, T], BF16,
                                            side="right"))
    V = sAB.enter_context(nc.sbuf_tensor(f"{tag}_V", [128, NTILE, 1040], BF16,
                                         side="right"))

    for qd in range(NQUAD):
        for s in range(4):
            t = qd * 4 + s
            lc_.ln_tile(t, t % 2)
            lc_.transpose_xn(s)
        xn_done = p.tok("v")
        for lb in range(LC):
            bk = pb[lb % 2]
            p.acquire(bk)
            for cc in range(8):
                p.op("t", "matmul", bk.t[:, :], wdkvT[:, cc, lb * 128:(lb + 1) * 128],
                     lc_.xnT[:, cc, :], start=(cc == 0), stop=(cc == 7),
                     w=(xn_done, tok_ydma))
            bk.guard = p.op("s", "copy", ckvT[:, lb, qd * 512:(qd + 1) * 512],
                            bk.t[:, :], w=(p.ctr["t"],))
        for lb in range(QLC):
            bk = pb[2 + lb % 2]
            p.acquire(bk)
            for cc in range(8):
                p.op("t", "matmul", bk.t[:, :], wdqT[:, cc, lb * 128:(lb + 1) * 128],
                     lc_.xnT[:, cc, :], start=(cc == 0), stop=(cc == 7))
            bk.guard = p.op("s", "copy", t1T[:, lb, qd * 512:(qd + 1) * 512],
                            bk.t[:, :], w=(p.ctr["t"],))
        ck_done = p.tok("s")
        for s in range(4):
            kt = qd * 4 + s
            p.op("v", "memset", V[:, kt, 64:1040:65], 1.0)
            for grp in range(2):
                bk = pb[4 + grp]
                p.acquire(bk)
                for lcc in range(LC):
                    p.op("t", "matmul", bk.t[:, :],
                         ckvT[:, lcc, kt * 128:(kt + 1) * 128],
                         veff[:, lcc, grp * 512:(grp + 1) * 512],
                         start=(lcc == 0), stop=(lcc == LC - 1), w=(ck_done,))
                last = None
                for hh in range(8):
                    h = grp * 8 + hh
                    last = p.op("v", "tensor_copy", V[:, kt, h * 65:h * 65 + 64],
                                bk.t[:, hh * 64:(hh + 1) * 64], w=(p.ctr["t"],))
                bk.guard = last

    if dbg:
        p.tap("xnT", lc_.xnT[:, :, :], [128, 8, 512], dtype=BF16)
        p.tap("ckvT", ckvT[:, :, :], [128, LC, T], dtype=BF16)
        p.tap("t1T", t1T[:, :, :], [128, QLC, T], dtype=BF16)
        p.tap("V0", V[:, 0, :], [128, 1040], dtype=BF16)
        p.tap("mean", lc_.mean_all[:, :], [128, NTILE])
        p.tap("rstd", lc_.rstd_all[:, :], [128, NTILE])

    p.barrier()
    sA.close()

    sB = ExitStack()
    nsb = sB.enter_context(nc.sbuf_tensor(f"{tag}_nh", [128, QLC, KVL], BF16))
    qlat = sB.enter_context(nc.sbuf_tensor(f"{tag}_qlat", [128, 2, LC, 512], BF16))
    sqq = sB.enter_context(nc.sbuf_tensor(f"{tag}_sqq", [128, LC, 512], BF16))
    row_f = sB.enter_context(nc.sbuf_tensor(f"{tag}_rowf", [1, 512], F32))
    rmax = sB.enter_context(nc.sbuf_tensor(f"{tag}_rmax", [1, 1], F32))
    nbs_b = sB.enter_context(nc.sbuf_tensor(f"{tag}_nbsb", [1, 2], BF16))
    bcol = sB.enter_context(nc.sbuf_tensor(f"{tag}_bcol", [128, 2], F32))
    probsT = sB.enter_context(nc.sbuf_tensor(f"{tag}_probsT", [128, 2, 512], BF16))
    yaug = sB.enter_context(nc.sbuf_tensor(f"{tag}_yaug", [128, 2, 512], F32))
    rec_col = sB.enter_context(nc.sbuf_tensor(f"{tag}_rec", [128, 1], F32))
    yattn = sB.enter_context(nc.sbuf_tensor(f"{tag}_yattn", [128, 4, 1024], F32,
                                            side="right"))
    maxc = sB.enter_context(nc.sbuf_tensor(f"{tag}_maxc", [1, 1], F32))
    crow = sB.enter_context(nc.sbuf_tensor(f"{tag}_crow", [1, NQUAD], F32))

    probs_g = [None, None]
    yattn_g = [None]
    yaug_g = [None, None]

    for qd2 in range(NQUAD):
        bk = pb[5]
        p.acquire(bk)
        for lcc in range(LC):
            tk = p.op("s", "activation", sqq[:, 0, :],
                      ckvT[:, lcc, qd2 * 512:(qd2 + 1) * 512], AF.Square,
                      w=(p.ctr["t"],))
            p.op("t", "matmul", bk.t[0:1, :], ones_b[:, 0:1], sqq[:, 0, :],
                 start=(lcc == 0), stop=(lcc == LC - 1), w=(tk,))
        bk.guard = p.op("v", "reduce_max", crow[0:1, qd2:qd2 + 1],
                        bk.t[0:1, 0:512], AX.X, w=(p.ctr["t"],))
    p.fence("v")
    p.op("v", "reduce_max", maxc[0:1, 0:1], crow[0:1, :], AX.X)
    p.op("s", "activation", maxc[0:1, :], maxc[0:1, 0:1], AF.Sqrt,
         w=(p.ctr["v"],))
    maxc_done = p.tok("s")

    iters = [(qg, h) for qg in range(NQUAD) for h in range(NH)]
    NIT = len(iters)

    def emit_setup(i):
        qg, h = iters[i]
        pi = i % 2
        hb = (h % 2) * 64
        hc = h // 2
        for qlc in range(QLC):
            bk = pb[2 + qlc % 2]
            p.acquire(bk)
            p.op("t", "matmul", bk.t[:, 0:KVL],
                 wuq[hb:hb + 64, hc, qlc * 128:(qlc + 1) * 128],
                 keff[hb:hb + 64, hc, :], start=True, stop=True,
                 w=(maxc_done,))
            bk.guard = p.op("s", "activation", nsb[:, qlc, :], bk.t[:, 0:KVL],
                            AF.Copy, scale=0.125, w=(p.ctr["t"],))
        nh_done = p.tok("s")
        for lb in range(LC):
            bk = pb[2 + lb % 2]
            p.acquire(bk)
            for qlc in range(QLC):
                p.op("t", "matmul", bk.t[:, :],
                     nsb[:, qlc, lb * 128:(lb + 1) * 128],
                     t1T[:, qlc, qg * 512:(qg + 1) * 512],
                     start=(qlc == 0), stop=(qlc == QLC - 1), w=(nh_done,))
            p.op("s", "copy", qlat[:, pi, lb, :], bk.t[:, :], w=(p.ctr["t"],))
            p.fence("s")
            bk.guard = p.op("s", "activation", sqq[:, lb, :],
                            qlat[:, pi, lb, :], AF.Square, w=(p.ctr["t"],))
        sq_done = p.tok("s")
        bk_row = pb[2]
        p.acquire(bk_row)
        for lb in range(LC):
            p.op("t", "matmul", bk_row.t[0:1, :], ones_b[:, 0:1], sqq[:, lb, :],
                 start=(lb == 0), stop=(lb == LC - 1), w=(sq_done,))
        bk_row.guard = p.op("s", "activation", row_f[0:1, :],
                            bk_row.t[0:1, 0:512], AF.Sqrt,
                            w=(p.ctr["t"], p.ctr["v"]))
        p.op("v", "reduce_max", rmax[0:1, 0:1], row_f[0:1, :], AX.X,
             w=(bk_row.guard, p.ctr["s"]))
        p.fence("v")
        p.op("v", "tensor_scalar", rmax[0:1, 0:1], rmax[0:1, 0:1],
             maxc[0:1, 0:1], -1.0, ALU.mult, ALU.mult)
        tnb = p.op("s", "copy", nbs_b[0:1, pi:pi + 1], rmax[0:1, 0:1],
                   w=(p.ctr["v"], p.ctr["t"]))
        p.acquire(bk_row)
        p.op("t", "matmul", bk_row.t[:, 0:1], ones_b[0:1, :],
             nbs_b[0:1, pi:pi + 1], start=True, stop=True, w=(tnb,))
        bk_row.guard = p.op("v", "tensor_copy", bcol[:, pi:pi + 1],
                            bk_row.t[:, 0:1], w=(p.ctr["t"], p.ctr["s"]))
        return p.tok("v")

    def emit_finish(i):
        qg, h = iters[i]
        pi = i % 2
        ybank = pb[4 + pi]
        for sub in range(4):
            p.op("t", "transpose", ybank.t[:, sub * 128:sub * 128 + 65],
                 yaug[0:65, pi, sub * 128:(sub + 1) * 128],
                 ident_f[0:65, 0:65], w=(p.ctr["v"],))
            p.op("v", "reciprocal", rec_col[:, :],
                 ybank.t[:, sub * 128 + 64:sub * 128 + 65], w=(p.ctr["t"],))
            p.fence("v")
            p.op("v", "tensor_scalar",
                 yattn[:, sub, h * 64:(h + 1) * 64],
                 ybank.t[:, sub * 128:sub * 128 + 64], rec_col[:, :], None,
                 ALU.mult, w=(yattn_g[0],))
        ybank.guard = p.tok("v")
        if h == NH - 1:
            yattn_g[0] = p.op("y", "dma_start",
                              out=y_attn_d[qg * 512:(qg + 1) * 512, :].rearrange(
                                  "(s p) d -> p s d", p=128),
                              in_=yattn[:, :, :], w=(p.ctr["v"],))

    negb_done = emit_setup(0)
    for i in range(NIT):
        qg, h = iters[i]
        pi = i % 2
        nkt = 4 * qg + 4
        my_negb = negb_done

        ybank = pb[4 + pi]
        p.acquire(ybank)

        def scores(kt):
            bk = pb[kt % 2]
            p.acquire(bk)
            for lcc in range(LC):
                p.op("t", "matmul", bk.t[:, :],
                     ckvT[:, lcc, kt * 128:(kt + 1) * 128],
                     qlat[:, pi, lcc, :], start=(lcc == 0), stop=(lcc == LC - 1),
                     w=(my_negb,))
            if kt >= 4 * qg:
                wexp = p.op("v", "tensor_tensor", bk.t[:, :], bk.t[:, :],
                            cmask[:, kt - 4 * qg, :], ALU.add, w=(p.ctr["t"],))
            else:
                wexp = p.tok("t")
            ppi = kt % 2
            tke = p.op("s", "activation", probsT[:, ppi, :], bk.t[:, :], AF.Exp,
                       bias=bcol[:, pi:pi + 1], scale=1.0,
                       w=(wexp, probs_g[ppi], my_negb))
            probs_g[ppi] = None
            bk.guard = tke
            return tke

        def pv(kt, tke):
            ppi = kt % 2
            probs_g[ppi] = p.op("t", "matmul", ybank.t[0:65, :],
                                V[:, kt, h * 65:(h + 1) * 65], probsT[:, ppi, :],
                                start=(kt == 0), stop=(kt == nkt - 1), w=(tke,))

        tke_prev = scores(0)
        tke = scores(1)
        pv(0, tke_prev)
        tke_prev = tke
        if i + 1 < NIT:
            negb_done = emit_setup(i + 1)
        if i > 0:
            emit_finish(i - 1)
        for kt in range(2, nkt):
            tke = scores(kt)
            pv(kt - 1, tke_prev)
            tke_prev = tke
        pv(nkt - 1, tke_prev)

        ybank.guard = p.op("v", "tensor_copy", yaug[0:65, pi, :],
                           ybank.t[0:65, :],
                           w=(p.ctr["t"], yaug_g[pi]))
        yaug_g[pi] = p.tok("t")
    emit_finish(NIT - 1)

    p.barrier()
    sB.close()
    sAB.close()

    ffn_fn(p, io, lc_, dbg)
    p.barrier()



def ffn_l0_moe(p, io, lc_, dbg=False):
    nc = p.nc
    sb = io["sb"]
    ones_b = sb["ones_b"]
    W = io["w"]
    x_out, y_attn_d = io["x_out"], io["y_attn_d"]
    pb = lc_.pb

    sC = ExitStack()
    lc_.alloc_tiles(sC, with_f32=True)
    rw = sC.enter_context(nc.sbuf_tensor("l0_rw", [128, 8, 8], F32))
    rb_row = sC.enter_context(nc.sbuf_tensor("l0_rb", [1, 8], F32))
    w1 = sC.enter_context(nc.sbuf_tensor("l0_w1", [128, 2, 8, 512], BF16))
    w3 = sC.enter_context(nc.sbuf_tensor("l0_w3", [128, 2, 8, 512], BF16))
    w2 = sC.enter_context(nc.sbuf_tensor("l0_w2", [128, 2, 4, 1024], BF16))
    gts = sC.enter_context(nc.sbuf_tensor("l0_gts", [128, 4, 8], F32))
    acc = sC.enter_context(nc.sbuf_tensor("l0_acc", [128, 4, 1024], F32))
    hT = sC.enter_context(nc.sbuf_tensor("l0_hT", [128, 4, 512], BF16))
    gact = sC.enter_context(nc.sbuf_tensor("l0_gact", [128, 512], BF16))
    ya_sb = sC.enter_context(nc.sbuf_tensor("l0_ya", [128, D], F32))
    gm1 = sC.enter_context(nc.sbuf_tensor("l0_gm1", [128, 1], F32))
    gnm1 = sC.enter_context(nc.sbuf_tensor("l0_gnm1", [128, 1], F32))
    gmask = sC.enter_context(nc.sbuf_tensor("l0_gmask", [128, 8], F32))
    gl2 = sC.enter_context(nc.sbuf_tensor("l0_gl2", [128, 8], F32))
    gm2 = sC.enter_context(nc.sbuf_tensor("l0_gm2", [128, 1], F32))
    gtop = sC.enter_context(nc.sbuf_tensor("l0_gtop", [128, 8], F32))
    ge = sC.enter_context(nc.sbuf_tensor("l0_ge", [128, 8], F32))
    gsum = sC.enter_context(nc.sbuf_tensor("l0_gsum", [128, 1], F32))

    NEXP = 10
    wslot_g = [None, None]
    hT_g = [None]
    gact_g = [None]
    acc_g = [None]
    ya_g = [None]

    p.op("y", "dma_start", out=rw[:, :, :],
         in_=W["L0_ffn_router_w"].rearrange("(c p) e -> p c e", p=128))
    p.op("y", "dma_start", out=rb_row[0:1, :], in_=W["L0_ffn_router_b"][None, :])

    def load_expert(e, sl, wtok):
        if e < 8:
            a1, a3, a2 = (W["L0_ffn_routed_w1"][e], W["L0_ffn_routed_w3"][e],
                          W["L0_ffn_routed_w2"][e])
        else:
            a1, a3, a2 = (W["L0_ffn_shared_w1"][e - 8], W["L0_ffn_shared_w3"][e - 8],
                          W["L0_ffn_shared_w2"][e - 8])
        p.op("g", "dma_start", out=w1[:, sl, :, :],
             in_=a1.rearrange("(c p) h -> p c h", p=128), w=(wtok,))
        p.op("g", "dma_start", out=w3[:, sl, :, :],
             in_=a3.rearrange("(c p) h -> p c h", p=128))
        return p.op("g", "dma_start", out=w2[:, sl, :, :],
                    in_=a2.rearrange("(h p) d2 -> p h d2", p=128))

    wtoks = {}

    for qd in range(NQUAD):
        wtoks[0] = load_expert(0, 0, wslot_g[0])
        for s in range(4):
            t = qd * 4 + s
            xi = (t % 2) % lc_.nx
            lc_.xn_tile_from_stats(t, xi)
            tka = p.op("y", "dma_start", out=ya_sb[:, :],
                       in_=y_attn_d[t * 128:(t + 1) * 128, :], w=(ya_g[0],))
            tk = p.op("v", "tensor_add", acc[:, s, :], lc_.x_sb[:, xi, :],
                      ya_sb[:, :], w=(tka, acc_g[0]))
            lc_.xg[xi] = tk
            ya_g[0] = tk
            lc_.transpose_xn(s)
            lc_.transpose_xnf((pb[0], pb[1]))
            bk = pb[5]
            p.acquire(bk)
            for cc in range(8):
                p.op("t", "matmul", bk.t[:, 0:8],
                     lc_.xnT_f[:, cc, :], rw[:, cc, :],
                     start=(cc == 0), stop=False,
                     w=(p.ctr["v"], p.dctr["g"], p.dctr["y"]))
            p.op("t", "matmul", bk.t[:, 0:8], sb["ones_f"][0:1, :], rb_row[0:1, :],
                 start=False, stop=True)
            p.op("v", "reduce_max", gm1[:, :], bk.t[:, 0:8], AX.X, w=(p.ctr["t"],))
            p.op("v", "reduce_max", gnm1[:, :], bk.t[:, 0:8], AX.X, negate=True)
            p.fence("v")
            p.op("v", "tensor_scalar", gmask[:, :], bk.t[:, 0:8], gm1[:, :], None,
                 ALU.is_ge)
            p.fence("v")
            p.op("v", "scalar_tensor_tensor", gl2[:, :], gmask[:, :], NEG,
                 bk.t[:, 0:8], ALU.mult, ALU.add)
            p.fence("v")
            p.op("v", "reduce_max", gm2[:, :], gl2[:, :], AX.X)
            p.fence("v")
            bk.guard = p.op("v", "tensor_scalar", gtop[:, :], bk.t[:, 0:8],
                            gm2[:, :], None, ALU.is_ge)
            tke = p.op("s", "activation", ge[:, :], bk.t[:, 0:8], AF.Exp,
                       bias=gnm1[:, :], scale=1.0, w=(p.ctr["v"],))
            p.op("v", "tensor_mul", ge[:, :], ge[:, :], gtop[:, :], w=(tke,))
            p.fence("v")
            p.op("v", "reduce_sum", gsum[:, :], ge[:, :], AX.X)
            p.fence("v")
            p.op("v", "reciprocal", gsum[:, :], gsum[:, :])
            p.fence("v")
            p.op("v", "tensor_scalar", gts[:, s, :], ge[:, :], gsum[:, :], None,
                 ALU.mult)
        xn_done = p.tok("v")

        for e in range(NEXP):
            sl = e % 2
            if e + 1 < NEXP:
                wtoks[e + 1] = load_expert(e + 1, (e + 1) % 2,
                                           wslot_g[(e + 1) % 2])
            for Hc in range(4):
                bkG = pb[Hc % 2]
                bkU = pb[2 + Hc % 2]
                p.acquire(bkG)
                p.acquire(bkU)
                for cc in range(8):
                    p.op("t", "matmul", bkG.t[:, :],
                         w1[:, sl, cc, Hc * 128:(Hc + 1) * 128],
                         lc_.xnT[:, cc, :], start=(cc == 0), stop=(cc == 7),
                         w=(wtoks[e], xn_done))
                for cc in range(8):
                    p.op("t", "matmul", bkU.t[:, :],
                         w3[:, sl, cc, Hc * 128:(Hc + 1) * 128],
                         lc_.xnT[:, cc, :], start=(cc == 0), stop=(cc == 7))
                tks = p.op("s", "activation", gact[:, :], bkG.t[:, :], AF.Silu,
                           w=(p.ctr["t"], gact_g[0]))
                bkG.guard = tks
                tkv = p.op("v", "tensor_mul", hT[:, Hc, :], gact[:, :],
                           bkU.t[:, :], w=(tks, hT_g[0]))
                bkU.guard = tkv
                gact_g[0] = tkv
            h_done = p.tok("v")
            for sub in range(4):
                for grp in range(2):
                    bk = pb[4 + (sub * 2 + grp) % 2]
                    p.acquire(bk)
                    for Hc in range(4):
                        p.op("t", "matmul", bk.t[:, :],
                             hT[:, Hc, sub * 128:(sub + 1) * 128],
                             w2[:, sl, Hc, grp * 512:(grp + 1) * 512],
                             start=(Hc == 0), stop=(Hc == 3), w=(h_done,))
                    if e < 8:
                        bk.guard = p.op("v", "scalar_tensor_tensor",
                                        acc[:, sub, grp * 512:(grp + 1) * 512],
                                        bk.t[:, :], gts[:, sub, e:e + 1],
                                        acc[:, sub, grp * 512:(grp + 1) * 512],
                                        ALU.mult, ALU.add, w=(p.ctr["t"],))
                    else:
                        bk.guard = p.op("v", "tensor_add",
                                        acc[:, sub, grp * 512:(grp + 1) * 512],
                                        bk.t[:, :],
                                        acc[:, sub, grp * 512:(grp + 1) * 512],
                                        w=(p.ctr["t"],))
            hT_g[0] = p.tok("t")
            wslot_g[sl] = p.tok("t")
        acc_g[0] = p.op("y", "dma_start",
                        out=x_out[qd * 512:(qd + 1) * 512, :].rearrange(
                            "(s p) d -> p s d", p=128),
                        in_=acc[:, :, :], w=(p.ctr["v"],))

    if dbg:
        p.tap("gts_last", gts[:, :, :], [128, 4, 8])

    sC.close()



def ffn_l2_mixer(p, io, lc_, dbg=False):
    nc = p.nc
    W = io["w"]
    sb = io["sb"]
    x_out, y_attn_d = io["x_out"], io["y_attn_d"]
    pb = lc_.pb

    sC = ExitStack()
    lc_.alloc_tiles(sC)
    wrT = sC.enter_context(nc.sbuf_tensor("l2_wrT", [128, 8, 1024], BF16))
    wkT = sC.enter_context(nc.sbuf_tensor("l2_wkT", [128, 8, 1024], BF16))
    wvT = sC.enter_context(nc.sbuf_tensor("l2_wvT", [128, 8, 1024], BF16))
    wr1m = sC.enter_context(nc.sbuf_tensor("l2_wr1m", [128, 8, 1024], BF16))
    wk1m = sC.enter_context(nc.sbuf_tensor("l2_wk1m", [128, 8, 1024], BF16))
    mus = sC.enter_context(nc.sbuf_tensor("l2_mus", [128, 4, 8], F32))
    xq = sC.enter_context(nc.sbuf_tensor("l2_xq", [128, 2, 8, 513], BF16))
    kact = sC.enter_context(nc.sbuf_tensor("l2_kact", [128, 8, 512], BF16))
    rsig = sC.enter_context(nc.sbuf_tensor("l2_rsig", [128, 4, 1024], F32))
    tmp = sC.enter_context(nc.sbuf_tensor("l2_tmp", [128, 512], F32))
    acc = sC.enter_context(nc.sbuf_tensor("l2_acc", [128, 4, 1024], F32))
    ya_sb = sC.enter_context(nc.sbuf_tensor("l2_ya", [128, D], F32))

    wr_d = nc.dram_tensor("l2_wr_d", [1024, 1024], BF16, kind="Internal").ap()
    wk_d = nc.dram_tensor("l2_wk_d", [1024, 1024], BF16, kind="Internal").ap()
    wv_d = nc.dram_tensor("l2_wv_d", [1024, 1024], BF16, kind="Internal").ap()

    t1 = p.op("g", "dma_start", out=wr_d, in_=W["L2_ffn_W_r"])
    t2 = p.op("g", "dma_start", out=wk_d, in_=W["L2_ffn_W_k"])
    t3 = p.op("g", "dma_start", out=wv_d, in_=W["L2_ffn_W_v"])
    tmu = p.op("g", "dma_start", out=mus[:, 0, :], in_=W["L2_ffn_mu_r_pc"])
    tmu = p.op("g", "dma_start", out=mus[:, 2, :], in_=W["L2_ffn_mu_k_pc"])
    for cc in range(8):
        p.op("y", "dma_start", out=wrT[:, cc, :],
             in_=wr_d[:, cc * 128:(cc + 1) * 128], transpose=True, w=(t1,))
        p.op("y", "dma_start", out=wkT[:, cc, :],
             in_=wk_d[:, cc * 128:(cc + 1) * 128], transpose=True, w=(t2,))
        p.op("y", "dma_start", out=wvT[:, cc, :],
             in_=wv_d[:, cc * 128:(cc + 1) * 128], transpose=True, w=(t3,))
    wT_tok = p.dtok("y")
    p.op("v", "tensor_scalar", mus[:, 1, :], mus[:, 0, :], -1.0, 1.0,
         ALU.mult, ALU.add, w=(tmu,))
    p.op("v", "tensor_scalar", mus[:, 3, :], mus[:, 2, :], -1.0, 1.0,
         ALU.mult, ALU.add)
    p.fence("v")
    for cc in range(8):
        p.op("v", "tensor_scalar", wr1m[:, cc, :], wrT[:, cc, :],
             mus[:, 1, cc:cc + 1], None, ALU.mult, w=(wT_tok,))
        p.op("v", "tensor_scalar", wk1m[:, cc, :], wkT[:, cc, :],
             mus[:, 3, cc:cc + 1], None, ALU.mult)
        p.fence("v")
        p.op("v", "tensor_scalar", wrT[:, cc, :], wrT[:, cc, :],
             mus[:, 0, cc:cc + 1], None, ALU.mult)
        p.op("v", "tensor_scalar", wkT[:, cc, :], wkT[:, cc, :],
             mus[:, 2, cc:cc + 1], None, ALU.mult)
        p.fence("v")
    wm_done = p.tok("v")
    p.op("v", "memset", xq[:, :, :, 0:1], 0.0)
    p.fence("v")

    acc_g = [None]
    ya_g = [None]
    mix_g = [None, None]

    for qd in range(NQUAD):
        par = qd % 2
        for s in range(4):
            t = qd * 4 + s
            xi = (t % 2) % lc_.nx
            lc_.xn_tile_from_stats(t, xi)
            tka = p.op("y", "dma_start", out=ya_sb[:, :],
                       in_=y_attn_d[t * 128:(t + 1) * 128, :], w=(ya_g[0],))
            tk = p.op("v", "tensor_add", acc[:, s, :], lc_.x_sb[:, xi, :],
                      ya_sb[:, :], w=(tka, acc_g[0]))
            lc_.xg[xi] = tk
            ya_g[0] = tk
            p.acquire(lc_.ptr_bank)
            for dc in range(8):
                p.op("t", "transpose", lc_.ptr[:, dc * 128:(dc + 1) * 128],
                     lc_.xn_tok[:, dc * 128:(dc + 1) * 128],
                     sb["ident_b"][:, :], w=(p.ctr["v"], p.ctr["s"]))
            lc_.ptr_bank.guard = p.op(
                "v", "tensor_copy", xq[:, par, :, 1 + s * 128:1 + (s + 1) * 128],
                lc_.ptr[:, :].rearrange("p (c m) -> p c m", c=8),
                w=(p.ctr["t"], mix_g[par]))
        p.fence("v")
        p.op("v", "tensor_copy", xq[:, 1 - par, :, 0:1], xq[:, par, :, 512:513],
             w=(mix_g[1 - par],))
        xn_done = p.tok("v")

        for s in range(4):
            for grp in range(2):
                bk = pb[(s * 2 + grp) % 2]
                p.acquire(bk)
                for cc in range(8):
                    p.op("t", "matmul", bk.t[:, :],
                         xq[:, par, cc, 1 + s * 128:1 + (s + 1) * 128],
                         wrT[:, cc, grp * 512:(grp + 1) * 512],
                         start=(cc == 0), stop=False, w=(xn_done, wm_done))
                for cc in range(8):
                    p.op("t", "matmul", bk.t[:, :],
                         xq[:, par, cc, s * 128:s * 128 + 128],
                         wr1m[:, cc, grp * 512:(grp + 1) * 512],
                         start=False, stop=(cc == 7))
                bk.guard = p.op("s", "activation",
                                rsig[:, s, grp * 512:(grp + 1) * 512],
                                bk.t[:, :], AF.Sigmoid,
                                w=(p.ctr["t"], p.ctr["v"]))
        for dc in range(8):
            bk = pb[2 + dc % 2]
            p.acquire(bk)
            for cc in range(8):
                p.op("t", "matmul", bk.t[:, :],
                     wkT[:, cc, dc * 128:(dc + 1) * 128],
                     xq[:, par, cc, 1:513], start=(cc == 0), stop=False,
                     w=(xn_done, wm_done))
            for cc in range(8):
                p.op("t", "matmul", bk.t[:, :],
                     wk1m[:, cc, dc * 128:(dc + 1) * 128],
                     xq[:, par, cc, 0:512], start=False, stop=(cc == 7))
            p.fence("v")
            p.op("v", "tensor_scalar_max", tmp[:, :], bk.t[:, :], 0.0,
                 w=(p.ctr["t"], p.ctr["s"]))
            bk.guard = p.tok("v")
            p.fence("v")
            p.op("s", "activation", kact[:, dc, :], tmp[:, :], AF.Square,
                 w=(p.ctr["v"], p.ctr["t"]))
        kact_done = p.tok("s")
        for s in range(4):
            for grp in range(2):
                bk = pb[4 + (s * 2 + grp) % 2]
                p.acquire(bk)
                for cc in range(8):
                    p.op("t", "matmul", bk.t[:, :],
                         kact[:, cc, s * 128:(s + 1) * 128],
                         wvT[:, cc, grp * 512:(grp + 1) * 512],
                         start=(cc == 0), stop=(cc == 7),
                         w=(kact_done, wT_tok))
                p.fence("v")
                p.op("v", "tensor_mul", tmp[:, :], bk.t[:, :],
                     rsig[:, s, grp * 512:(grp + 1) * 512], w=(p.ctr["t"],))
                bk.guard = p.tok("v")
                p.fence("v")
                p.op("v", "tensor_add", acc[:, s, grp * 512:(grp + 1) * 512],
                     acc[:, s, grp * 512:(grp + 1) * 512], tmp[:, :])
        mix_g[par] = p.tok("t")
        acc_g[0] = p.op("y", "dma_start",
                        out=x_out[qd * 512:(qd + 1) * 512, :].rearrange(
                            "(s p) d -> p s d", p=128),
                        in_=acc[:, :, :], w=(p.ctr["v"],))

    sC.close()


def emit_l0(p, io, dbg=False):
    emit_mla_layer(p, io, {"tag": "l0", "wp": "L0_attn", "QL": 512, "KVL": 512},
                   ffn_l0_moe, dbg)


def emit_l2(p, io, dbg=False):
    emit_mla_layer(p, io, {"tag": "l2", "wp": "L2_attn", "QL": 128, "KVL": 128},
                   ffn_l2_mixer, dbg)



def emit_l1(p, io, dbg=False):
    import os as _os
    """MLA + RoPE + LOBO-softmax attention, dense-MoE(8)+shared-swiglu FFN.
    scores^T scheme with Cauchy-Schwarz bound; exact row-max (needed for the
    lobo denominator) recovered as E = max_k exp(s-b) via PE-transposes of a
    running elementwise max of the prob tiles."""
    nc, es = p.nc, p.es
    sb = io["sb"]
    ident_b, ident_f = sb["ident_b"], sb["ident_f"]
    ones_b = sb["ones_b"]
    cmask = sb["cmask"]
    x_in, x_out, y_attn_d = io["x_in"], io["x_out"], io["y_attn_d"]
    W = io["w"]
    NH = 16

    lc_ = LayerCommon(p, sb, x_in, "l1")
    pb = lc_.pb
    ptrb = lc_.ptr
    SC = 1.0 / np.sqrt(96.0)

    qr_d = nc.dram_tensor("l1_qr_d", [T, 512], BF16, kind="Internal").ap()
    qcT_d = nc.dram_tensor("l1_qcT_d", [1024, T], BF16, kind="Internal").ap()
    if _os.environ.get("L1_QRCHK"):
        qrchk_d = nc.dram_tensor("tap_l1_qrchk", [T, 512], BF16,
                                 kind="ExternalOutput").ap()
        p.taps["l1_qrchk"] = [T, 512]
    else:
        qrchk_d = None

    sAB = es.enter_context(ExitStack())
    qc_st = sAB.enter_context(nc.sbuf_tensor("l1_qcst", [128, 512], BF16))
    latT = sAB.enter_context(nc.sbuf_tensor("l1_latT", [128, 2, T], BF16, side="right"))
    V = sAB.enter_context(nc.sbuf_tensor("l1_V", [128, NTILE, 1040], BF16, side="right"))
    kcat = sAB.enter_context(nc.sbuf_tensor("l1_kcat", [128, 2, T], BF16))
    wku = sAB.enter_context(nc.sbuf_tensor("l1_wku", [128, 2, 1024], BF16))

    sA = es.enter_context(ExitStack())
    wq = sA.enter_context(nc.sbuf_tensor("l1_wq", [128, 8, 1024], BF16))
    wqr = sA.enter_context(nc.sbuf_tensor("l1_wqr", [128, 8, 512], BF16))
    wkr = sA.enter_context(nc.sbuf_tensor("l1_wkr", [128, 8, 32], BF16))
    wkvd = sA.enter_context(nc.sbuf_tensor("l1_wkvd", [128, 8, 256], BF16))
    rtab = sA.enter_context(nc.sbuf_tensor("l1_rtab", [128, 1, 544], F32))
    ropeA = sA.enter_context(nc.sbuf_tensor("l1_ropeA", [128, 256], F32))
    ropeB = sA.enter_context(nc.sbuf_tensor("l1_ropeB", [128, 256], F32))
    ropeC = sA.enter_context(nc.sbuf_tensor("l1_ropeC", [128, 256], F32))
    ropeD = sA.enter_context(nc.sbuf_tensor("l1_ropeD", [128, 256], F32))
    ropeE = sA.enter_context(nc.sbuf_tensor("l1_ropeE", [128, 512], F32))
    qr_bf = sA.enter_context(nc.sbuf_tensor("l1_qrbf", [128, 512], BF16))
    kr_bf = sA.enter_context(nc.sbuf_tensor("l1_krbf", [128, 32], BF16))
    lc_.alloc_tiles(sA, nx=1)

    g = lambda out, in_, **kw: p.op("g", "dma_start", out=out, in_=in_, **kw)
    tok_wq = g(wq[:, :, :], W["L1_attn_Wq"].rearrange("(c p) m -> p c m", p=128))
    tok_wqr = g(wqr[:, :, :], W["L1_attn_Wqr"].rearrange("(c p) m -> p c m", p=128))
    tok_wkr = g(wkr[:, :, :], W["L1_attn_Wkr"].rearrange("(c p) m -> p c m", p=128))
    tok_wkvd = g(wkvd[:, :, :], W["L1_attn_Wkvd"].rearrange("(c p) m -> p c m", p=128))
    tok_wku = g(wku[:, :, :], W["L1_attn_Wku"].rearrange("(c p) m -> p c m", p=128))

    rtab_g = [None, None]
    qrst_g = [None]
    for qd in range(NQUAD):
        for s in range(4):
            t = qd * 4 + s
            ri = 0
            lc_.ln_tile(t, t % 2)
            lc_.transpose_xn(s)
            ttab = p.op("y", "dma_start", out=rtab[:, ri, :],
                        in_=W["r1tab"][t * 128:(t + 1) * 128, :], w=(rtab_g[ri],))
            xn_t = p.tok("v")
            bk = pb[4]
            p.acquire(bk)
            for cc in range(8):
                p.op("t", "matmul", bk.t[:, :], lc_.xnT[:, cc, s * 128:(s + 1) * 128],
                     wqr[:, cc, :], start=(cc == 0), stop=(cc == 7),
                     w=(xn_t, tok_wqr))
            ps3 = bk.t[:, :].rearrange("p (h d) -> p h d", h=16)
            qc3 = rtab[:, ri, 0:256].rearrange("p (h d) -> p h d", h=16)
            qs3 = rtab[:, ri, 256:512].rearrange("p (h d) -> p h d", h=16)
            qrb3 = ropeE[:, :].rearrange("p (h d) -> p h d", h=16)
            ra3 = ropeA[:, :].rearrange("p (h d) -> p h d", h=16)
            rb3 = ropeB[:, :].rearrange("p (h d) -> p h d", h=16)
            rc3 = ropeC[:, :].rearrange("p (h d) -> p h d", h=16)
            rd3 = ropeD[:, :].rearrange("p (h d) -> p h d", h=16)
            p.op("v", "tensor_mul", ra3[:, :, :], ps3[:, :, 0:16], qc3[:, :, :],
                 w=(p.ctr["t"], ttab, qrst_g[0]))
            p.op("v", "tensor_mul", rb3[:, :, :], ps3[:, :, 16:32], qs3[:, :, :])
            p.op("v", "tensor_mul", rc3[:, :, :], ps3[:, :, 16:32], qc3[:, :, :])
            p.op("v", "tensor_mul", rd3[:, :, :], ps3[:, :, 0:16], qs3[:, :, :])
            p.fence("v")
            p.op("v", "tensor_sub", qrb3[:, :, 0:16], ra3[:, :, :], rb3[:, :, :])
            tkq = p.op("v", "tensor_add", qrb3[:, :, 16:32], rc3[:, :, :], rd3[:, :, :])
            p.fence("v")
            tkq = p.op("v", "tensor_copy", qr_bf[:, :], ropeE[:, :], w=(tkq,))
            p.fence("v")
            bk.guard = tkq
            tdq = p.op("y", "dma_start", out=qr_d[t * 128:(t + 1) * 128, :],
                       in_=qr_bf[:, :], w=(tkq,))
            qrst_g[0] = tdq
            if qrchk_d is not None:
                tkchk = p.op("s", "copy", kr_bf[:, :] if False else lc_.sq_scr[:, 0:512],
                             qr_bf[:, :], w=(tkq,))
                p.op("y", "dma_start", out=qrchk_d[t * 128:(t + 1) * 128, :],
                     in_=lc_.sq_scr[:, 0:512], w=(tkchk,))
            if _os.environ.get("L1_SLOWA"):
                p.barrier()
            bk2 = pb[5]
            p.acquire(bk2)
            for cc in range(8):
                p.op("t", "matmul", bk2.t[:, 0:32],
                     lc_.xnT[:, cc, s * 128:(s + 1) * 128], wkr[:, cc, :],
                     start=(cc == 0), stop=(cc == 7), w=(tok_wkr,))
            kc1 = rtab[:, ri, 512:528]
            ks1 = rtab[:, ri, 528:544]
            p.op("v", "tensor_mul", ropeA[:, 0:16], bk2.t[:, 0:16], kc1,
                 w=(p.ctr["t"],))
            p.op("v", "tensor_mul", ropeB[:, 0:16], bk2.t[:, 16:32], ks1)
            p.op("v", "tensor_mul", ropeC[:, 0:16], bk2.t[:, 16:32], kc1)
            p.op("v", "tensor_mul", ropeD[:, 0:16], bk2.t[:, 0:16], ks1)
            p.fence("v")
            p.op("v", "tensor_sub", kr_bf[:, 0:16], ropeA[:, 0:16], ropeB[:, 0:16])
            tkk = p.op("v", "tensor_add", kr_bf[:, 16:32], ropeC[:, 0:16],
                       ropeD[:, 0:16])
            p.fence("v")
            bk2.guard = tkk
            p.acquire(lc_.ptr_bank)
            p.op("t", "transpose", ptrb[0:32, 0:128], kr_bf[:, :], ident_b[:, :],
                 w=(tkk,))
            p.op("v", "tensor_copy", kcat[64:96, 0, t * 128:(t + 1) * 128],
                 ptrb[0:32, 0:128], w=(p.ctr["t"],))
            lc_.ptr_bank.guard = p.op(
                "v", "tensor_copy", kcat[64:96, 1, t * 128:(t + 1) * 128],
                ptrb[0:32, 0:128])
            rtab_g[ri] = p.tok("v")
        xn_done = p.tok("v")
        for lb in range(2):
            bk = pb[lb]
            p.acquire(bk)
            for cc in range(8):
                p.op("t", "matmul", bk.t[:, :], wkvd[:, cc, lb * 128:(lb + 1) * 128],
                     lc_.xnT[:, cc, :], start=(cc == 0), stop=(cc == 7),
                     w=(xn_done, tok_wkvd))
            bk.guard = p.op("s", "copy", latT[:, lb, qd * 512:(qd + 1) * 512],
                            bk.t[:, :], w=(p.ctr["t"],))
        qc_g = [None]
        for eb in range(8):
            bk = pb[2 + eb % 2]
            p.acquire(bk)
            for cc in range(8):
                p.op("t", "matmul", bk.t[:, :], wq[:, cc, eb * 128:(eb + 1) * 128],
                     lc_.xnT[:, cc, :], start=(cc == 0), stop=(cc == 7),
                     w=(tok_wq,))
            bk.guard = p.op("s", "activation", qc_st[:, :],
                            bk.t[:, :], AF.Copy, scale=SC,
                            w=(p.ctr["t"], qc_g[0]))
            qc_g[0] = p.op("y", "dma_start",
                           out=qcT_d[eb * 128:(eb + 1) * 128,
                                     qd * 512:(qd + 1) * 512],
                           in_=qc_st[:, :], w=(bk.guard,))
    p.barrier()
    sA.close()

    sA2 = ExitStack()
    wvu = sA2.enter_context(nc.sbuf_tensor("l1_wvu", [128, 2, 1024], BF16))
    tok_wvu = g(wvu[:, :, :], W["L1_attn_Wvu"].rearrange("(c p) m -> p c m", p=128))
    for kt in range(NTILE):
        p.op("v", "memset", V[:, kt, 64:1040:65], 1.0)
        for grp in range(2):
            bk = pb[4 + grp]
            p.acquire(bk)
            for lb in range(2):
                p.op("t", "matmul", bk.t[:, :],
                     latT[:, lb, kt * 128:(kt + 1) * 128],
                     wvu[:, lb, grp * 512:(grp + 1) * 512],
                     start=(lb == 0), stop=(lb == 1),
                     w=(tok_wvu,))
            last = None
            for hh in range(8):
                h = grp * 8 + hh
                last = p.op("v", "tensor_copy", V[:, kt, h * 65:h * 65 + 64],
                            bk.t[:, hh * 64:(hh + 1) * 64], w=(p.ctr["t"],))
            bk.guard = last
    p.barrier()
    sA2.close()

    sB = ExitStack()
    qcat = sB.enter_context(nc.sbuf_tensor("l1_qcat", [128, 2, 512], BF16))
    qr_sb = sB.enter_context(nc.sbuf_tensor("l1_qrsb", [128, 2, 4, 512], BF16))
    Mrow = sB.enter_context(nc.sbuf_tensor("l1_M", [128, 2, 512], BF16))
    sq96 = sB.enter_context(nc.sbuf_tensor("l1_sq96", [128, 512], BF16))
    row_f = sB.enter_context(nc.sbuf_tensor("l1_rowf", [1, 512], F32))
    rmax1 = sB.enter_context(nc.sbuf_tensor("l1_rmax", [1, 1], F32))
    nbs1_b = sB.enter_context(nc.sbuf_tensor("l1_nbsb", [1, 2], BF16))
    bcol1 = sB.enter_context(nc.sbuf_tensor("l1_bcol", [128, 2], F32))
    probsT = sB.enter_context(nc.sbuf_tensor("l1_probsT", [128, 2, 512], BF16))
    yaug = sB.enter_context(nc.sbuf_tensor("l1_yaug", [128, 2, 512], F32))
    rec_col = sB.enter_context(nc.sbuf_tensor("l1_rec", [128, 1], F32))
    dcol = sB.enter_context(nc.sbuf_tensor("l1_dcol", [128, 1], F32))
    ecol = sB.enter_context(nc.sbuf_tensor("l1_ecol", [128, 1], F32))
    yattn = sB.enter_context(nc.sbuf_tensor("l1_yattn", [128, 4, 1024], F32,
                                            side="right"))
    crow = sB.enter_context(nc.sbuf_tensor("l1_crow", [1, NQUAD], F32))
    srow = sB.enter_context(nc.sbuf_tensor("l1_srow", [1, 16], F32))
    frow = sB.enter_context(nc.sbuf_tensor("l1_frow", [1, 16], F32))
    maxlat = sB.enter_context(nc.sbuf_tensor("l1_maxlat", [1, 1], F32))
    maxkr = sB.enter_context(nc.sbuf_tensor("l1_maxkr", [1, 1], F32))
    Crow = sB.enter_context(nc.sbuf_tensor("l1_Crow", [1, 16], BF16))
    Cb = sB.enter_context(nc.sbuf_tensor("l1_Cb", [128, 16], F32))
    sqw = sB.enter_context(nc.sbuf_tensor("l1_sqw", [128, 64], BF16))

    probs_g = [None, None]
    yattn_g = [None]
    yaug_g = [None]
    qcat_g = [None, None]
    kcat_g = [None]
    qr_g = [None]

    if dbg and "l1_qrd" in p.want_taps:
        qrd_tap = nc.dram_tensor("tap_l1_qrd", [512, 512], BF16,
                                 kind="ExternalOutput").ap()
        p.taps["l1_qrd"] = [512, 512]
        lastq = None
        for tt_ in range(4):
            tk_ = p.op("y", "dma_start", out=qcat[0:128, 0:512],
                       in_=qr_d[tt_ * 128:(tt_ + 1) * 128, :], w=(lastq,))
            lastq = p.op("y", "dma_start",
                         out=qrd_tap[tt_ * 128:(tt_ + 1) * 128, :],
                         in_=qcat[0:128, 0:512], w=(tk_,))
        p.waitonly("y", lastq)

    tkl = p.op("y", "dma_start", out=row_f[0:1, 0:16],
               in_=W["L1_attn_lobo_log"][None, :])
    p.op("s", "activation", Crow[0:1, :], row_f[0:1, 0:16], AF.Exp, w=(tkl,))
    bk = pb[0]
    p.acquire(bk)
    p.op("t", "matmul", bk.t[:, 0:16], ones_b[0:1, :], Crow[0:1, :],
         start=True, stop=True, w=(p.ctr["s"],))
    bk.guard = p.op("v", "tensor_copy", Cb[:, :], bk.t[:, 0:16], w=(p.ctr["t"],))

    for qd2 in range(NQUAD):
        bk = pb[5]
        p.acquire(bk)
        for lb in range(2):
            tk = p.op("s", "activation", sq96[:, :],
                      latT[:, lb, qd2 * 512:(qd2 + 1) * 512], AF.Square,
                      w=(p.ctr["t"],))
            p.op("t", "matmul", bk.t[0:1, :], ones_b[:, 0:1], sq96[:, :],
                 start=(lb == 0), stop=(lb == 1), w=(tk,))
        bk.guard = p.op("v", "reduce_max", crow[0:1, qd2:qd2 + 1],
                        bk.t[0:1, 0:512], AX.X, w=(p.ctr["t"],))
    p.fence("v")
    p.op("v", "reduce_max", maxlat[0:1, :], crow[0:1, :], AX.X)
    for qd2 in range(NQUAD):
        bk = pb[5]
        p.acquire(bk)
        tk = p.op("s", "activation", sq96[0:32, :],
                  kcat[64:96, 0, qd2 * 512:(qd2 + 1) * 512], AF.Square,
                  w=(p.ctr["t"], p.ctr["v"]))
        p.op("t", "matmul", bk.t[0:1, :], ones_b[0:32, 0:1], sq96[0:32, :],
             start=True, stop=True, w=(tk,))
        bk.guard = p.op("v", "reduce_max", crow[0:1, qd2:qd2 + 1],
                        bk.t[0:1, 0:512], AX.X, w=(p.ctr["t"],))
    p.fence("v")
    p.op("v", "reduce_max", maxkr[0:1, :], crow[0:1, :], AX.X)
    for h in range(NH):
        bk = pb[4]
        p.acquire(bk)
        for lb in range(2):
            tk = p.op("s", "activation", sqw[:, :], wku[:, lb, h * 64:(h + 1) * 64],
                      AF.Square, w=(p.ctr["t"], tok_wku))
            p.op("t", "matmul", bk.t[0:1, 0:64], ones_b[:, 0:1], sqw[:, :],
                 start=(lb == 0), stop=(lb == 1), w=(tk,))
        bk.guard = p.op("v", "reduce_sum", frow[0:1, h:h + 1], bk.t[0:1, 0:64],
                        AX.X, w=(p.ctr["t"],))
    p.fence("v")
    p.op("v", "tensor_scalar", srow[0:1, :], frow[0:1, :], maxlat[0:1, 0:1],
         maxkr[0:1, 0:1], ALU.mult, ALU.add)
    p.op("s", "activation", srow[0:1, :], srow[0:1, :], AF.Sqrt, w=(p.ctr["v"],))
    pre_done = p.tok("s")

    iters1 = [(qg, h) for qg in range(NQUAD) for h in range(NH)]
    NIT1 = len(iters1)
    qr_g2 = [None, None]

    def emit_setup1(i):
        qg, h = iters1[i]
        pi = i % 2
        if h == 0:
            qr_g2[qg % 2] = p.op(
                "y", "dma_start", out=qr_sb[:, qg % 2, :, :],
                in_=qr_d[qg * 512:(qg + 1) * 512, :].rearrange(
                    "(s p) d -> p s d", p=128), w=(p.ctr["t"],))
        qr_tok = qr_g2[qg % 2]
        for grp2 in range(qg + 1):
            bk = pb[grp2 % 2]
            p.acquire(bk)
            for lb in range(2):
                p.op("t", "matmul", bk.t[0:64, :],
                     wku[:, lb, h * 64:(h + 1) * 64],
                     latT[:, lb, grp2 * 512:(grp2 + 1) * 512],
                     start=(lb == 0), stop=(lb == 1), w=(pre_done,))
            bk.guard = p.op("s", "copy",
                            kcat[0:64, pi, grp2 * 512:(grp2 + 1) * 512],
                            bk.t[0:64, :], w=(p.ctr["t"],))
        qcat_dma = p.op("y", "dma_start", out=qcat[0:64, pi, :],
                        in_=qcT_d[h * 64:(h + 1) * 64,
                                  qg * 512:(qg + 1) * 512],
                        w=(qcat_g[pi], qr_tok))
        for sub in range(4):
            p.acquire(lc_.ptr_bank)
            p.op("t", "transpose", ptrb[0:32, 0:128],
                 qr_sb[:, qg % 2, sub, h * 32:(h + 1) * 32], ident_b[:, :],
                 w=(p.ctr["v"], qr_tok))
            lc_.ptr_bank.guard = p.op(
                "v", "tensor_copy", qcat[64:96, pi, sub * 128:(sub + 1) * 128],
                ptrb[0:32, 0:128], w=(p.ctr["t"],))
        qcat_done = p.tok("v")
        tk = p.op("s", "activation", sq96[0:96, :], qcat[0:96, pi, :],
                  AF.Square, w=(qcat_done, qcat_dma, p.ctr["t"]))
        bk_row = pb[0]
        p.acquire(bk_row)
        p.op("t", "matmul", bk_row.t[0:1, :], ones_b[0:96, 0:1], sq96[0:96, :],
             start=True, stop=True, w=(tk,))
        bk_row.guard = p.op("s", "activation", row_f[0:1, :],
                            bk_row.t[0:1, 0:512], AF.Sqrt,
                            w=(p.ctr["t"], p.ctr["v"]))
        p.op("v", "reduce_max", rmax1[0:1, 0:1], row_f[0:1, :], AX.X,
             w=(bk_row.guard, p.ctr["s"]))
        p.fence("v")
        p.op("v", "tensor_scalar", rmax1[0:1, 0:1], rmax1[0:1, 0:1],
             srow[0:1, h:h + 1], -1.0, ALU.mult, ALU.mult)
        tnb = p.op("s", "copy", nbs1_b[0:1, pi:pi + 1], rmax1[0:1, 0:1],
                   w=(p.ctr["v"], p.ctr["t"]))
        bk2 = pb[1]
        p.acquire(bk2)
        p.op("t", "matmul", bk2.t[:, 0:1], ones_b[0:1, :],
             nbs1_b[0:1, pi:pi + 1], start=True, stop=True, w=(tnb,))
        bk2.guard = p.op("v", "tensor_copy", bcol1[:, pi:pi + 1],
                         bk2.t[:, 0:1], w=(p.ctr["t"], p.ctr["s"]))
        return {"negb": p.tok("v"), "kc": p.tok("s"), "qdma": qcat_dma}

    def emit_finish1(i, m_done):
        qg, h = iters1[i]
        pi = i % 2
        ybank = pb[4 + pi]
        for sub in range(4):
            p.acquire(lc_.ptr_bank)
            p.op("t", "transpose", ptrb[:, 0:128],
                 Mrow[:, pi, sub * 128:(sub + 1) * 128], ident_b[:, :],
                 w=(m_done, p.ctr["v"]))
            p.op("v", "reduce_max", ecol[:, :], ptrb[:, 0:128], AX.X,
                 w=(p.ctr["t"],))
            lc_.ptr_bank.guard = p.tok("v")
            p.op("t", "transpose", ybank.t[:, sub * 128:sub * 128 + 65],
                 yaug[0:65, pi, sub * 128:(sub + 1) * 128],
                 ident_f[0:65, 0:65], w=(p.ctr["v"],))
            p.op("v", "scalar_tensor_tensor", dcol[:, :], ecol[:, :],
                 Cb[:, h:h + 1], ybank.t[:, sub * 128 + 64:sub * 128 + 65],
                 ALU.mult, ALU.add, w=(p.ctr["t"],))
            p.fence("v")
            p.op("v", "reciprocal", rec_col[:, :], dcol[:, :])
            p.fence("v")
            p.op("v", "tensor_scalar",
                 yattn[:, sub, h * 64:(h + 1) * 64],
                 ybank.t[:, sub * 128:sub * 128 + 64], rec_col[:, :], None,
                 ALU.mult, w=(yattn_g[0],))
        ybank.guard = p.tok("v")
        if h == NH - 1:
            yattn_g[0] = p.op("y", "dma_start",
                              out=y_attn_d[qg * 512:(qg + 1) * 512, :].rearrange(
                                  "(s p) d -> p s d", p=128),
                              in_=yattn[:, :, :], w=(p.ctr["v"],))

    st1 = emit_setup1(0)
    fin_prev = None
    for i in range(NIT1):
        qg, h = iters1[i]
        pi = i % 2
        nkt = 4 * qg + 4
        my = st1

        ybank = pb[4 + pi]
        p.acquire(ybank)

        def scores(kt, first):
            bk = pb[2 + kt % 2]
            p.acquire(bk)
            p.op("t", "matmul", bk.t[:, :],
                 kcat[0:96, pi, kt * 128:(kt + 1) * 128],
                 qcat[0:96, pi, :], start=True, stop=True,
                 w=(my["negb"], my["kc"], my["qdma"]))
            if kt >= 4 * qg:
                wexp = p.op("v", "tensor_tensor", bk.t[:, :], bk.t[:, :],
                            cmask[:, kt - 4 * qg, :], ALU.add, w=(p.ctr["t"],))
            else:
                wexp = p.tok("t")
            ppi = kt % 2
            tke = p.op("s", "activation", probsT[:, ppi, :], bk.t[:, :], AF.Exp,
                       bias=bcol1[:, pi:pi + 1], scale=1.0,
                       w=(wexp, probs_g[ppi], my["negb"]))
            probs_g[ppi] = None
            bk.guard = tke
            if first:
                tkm = p.op("v", "tensor_copy", Mrow[:, pi, :], probsT[:, ppi, :],
                           w=(tke, p.ctr["t"]))
            else:
                tkm = p.op("v", "tensor_max", Mrow[:, pi, :], Mrow[:, pi, :],
                           probsT[:, ppi, :], w=(tke,))
            return tke, tkm

        def pv(kt, tks):
            ppi = kt % 2
            tkpv = p.op("t", "matmul", ybank.t[0:65, :],
                        V[:, kt, h * 65:(h + 1) * 65], probsT[:, ppi, :],
                        start=(kt == 0), stop=(kt == nkt - 1), w=(tks[0],))
            probs_g[ppi] = [tkpv, tks[1]]

        tks_prev = scores(0, True)
        tks = scores(1, False)
        pv(0, tks_prev)
        tks_prev = tks
        if i + 1 < NIT1:
            st1 = emit_setup1(i + 1)
        if fin_prev is not None:
            emit_finish1(*fin_prev)
        for kt in range(2, nkt):
            tks = scores(kt, False)
            pv(kt - 1, tks_prev)
            tks_prev = tks
        pv(nkt - 1, tks_prev)

        ybank.guard = p.op("v", "tensor_copy", yaug[0:65, pi, :],
                           ybank.t[0:65, :], w=(p.ctr["t"],))
        fin_prev = (i, tks_prev[1])
        qcat_g[pi] = p.tok("t")
    emit_finish1(*fin_prev)

    if dbg:
        p.tap("l1_latT", latT[:, :, :], [128, 2, T], dtype=BF16)
        p.tap("l1_V0", V[:, 0, :], [128, 1040], dtype=BF16)
        p.tap("l1_kcat", kcat[:, 0, :], [128, T], dtype=BF16)
        p.tap("l1_qcat", qcat[:, 0, :], [128, 512], dtype=BF16)
        p.tap("l1_yattn", yattn[:, :, :], [128, 4, 1024])
        p.tap("l1_M", Mrow[:, 0, :], [128, 512], dtype=BF16)
        p.tap("l1_srow", srow[0:1, :], [1, 16])
        p.tap("l1_Cb", Cb[:, :], [128, 16])
    p.barrier()
    sB.close()
    sAB.close()

    ffn_l1_moe(p, io, lc_, dbg)
    p.barrier()


def ffn_l1_moe(p, io, lc_, dbg=False):
    nc = p.nc
    sb = io["sb"]
    ones_b = sb["ones_b"]
    ident_b = sb["ident_b"]
    W = io["w"]
    x_in, x_out, y_attn_d = io["x_in"], io["x_out"], io["y_attn_d"]
    pb = lc_.pb
    ptrb = lc_.ptr
    HC = 11

    sC1 = ExitStack()
    wo = sC1.enter_context(nc.sbuf_tensor("l1_wo", [128, 8, 1024], BF16))
    ya_sb = sC1.enter_context(nc.sbuf_tensor("l1_ya", [128, D], F32))
    x_t = sC1.enter_context(nc.sbuf_tensor("l1_xt", [128, D], F32))
    ybf = sC1.enter_context(nc.sbuf_tensor("l1_ybf", [128, D], BF16))
    yT = sC1.enter_context(nc.sbuf_tensor("l1_yT", [128, 8, 128], BF16))
    ya2 = sC1.enter_context(nc.sbuf_tensor("l1_ya2", [128, D], F32))

    two = p.op("g", "dma_start", out=wo[:, :, :],
               in_=W["L1_attn_Wo"].rearrange("(c p) m -> p c m", p=128))
    ya_g = [None]
    x_g = [None]
    yT_g = [None]
    ya2_g = [None]
    if dbg and "l1_yad" in p.want_taps:
        yad_tap = nc.dram_tensor("tap_l1_yad", [T, D], F32,
                                 kind="ExternalOutput").ap()
        p.taps["l1_yad"] = [T, D]
    else:
        yad_tap = None
    for t in range(NTILE):
        tka = p.op("y", "dma_start", out=ya_sb[:, :],
                   in_=y_attn_d[t * 128:(t + 1) * 128, :], w=(ya_g[0],))
        if yad_tap is not None:
            tka = p.op("y", "dma_start", out=yad_tap[t * 128:(t + 1) * 128, :],
                       in_=ya_sb[:, :], w=(tka,))
        tkx = p.op("y", "dma_start", out=x_t[:, :],
                   in_=x_in[t * 128:(t + 1) * 128, :], w=(x_g[0],))
        tkc = p.op("s", "copy", ybf[:, :], ya_sb[:, :], w=(tka, yT_g[0]))
        ya_g[0] = tkc
        p.acquire(lc_.ptr_bank)
        for dc in range(8):
            p.op("t", "transpose", ptrb[:, dc * 128:(dc + 1) * 128],
                 ybf[:, dc * 128:(dc + 1) * 128], ident_b[:, :], w=(tkc,))
        lc_.ptr_bank.guard = p.op(
            "v", "tensor_copy", yT[:, :, :],
            ptrb[:, :].rearrange("p (c m) -> p c m", c=8), w=(p.ctr["t"],))
        yT_done = p.tok("v")
        for grp in range(2):
            bk = pb[4 + grp]
            p.acquire(bk)
            for cc in range(8):
                p.op("t", "matmul", bk.t[:, :], yT[:, cc, :],
                     wo[:, cc, grp * 512:(grp + 1) * 512],
                     start=(cc == 0), stop=(cc == 7), w=(yT_done, two))
            bk.guard = p.op("v", "tensor_add", ya2[:, grp * 512:(grp + 1) * 512],
                            x_t[:, grp * 512:(grp + 1) * 512], bk.t[:, :],
                            w=(p.ctr["t"], tkx, ya2_g[0]))
        x_g[0] = p.tok("v")
        yT_g[0] = p.tok("t")
        ya2_g[0] = p.op("y", "dma_start", out=y_attn_d[t * 128:(t + 1) * 128, :],
                        in_=ya2[:, :], w=(p.ctr["v"],))
    p.barrier()
    sC1.close()

    sC = ExitStack()
    lc_.alloc_tiles(sC, with_f32=True)
    gw = sC.enter_context(nc.sbuf_tensor("l1_gw", [128, 8, 8], F32))
    w1 = sC.enter_context(nc.sbuf_tensor("l1_w1", [128, 2, 8, 1408], BF16))
    w3 = sC.enter_context(nc.sbuf_tensor("l1_w3", [128, 2, 8, 1408], BF16))
    w2 = sC.enter_context(nc.sbuf_tensor("l1_w2", [128, 2, HC, 1024], BF16))
    gts = sC.enter_context(nc.sbuf_tensor("l1_gts", [128, 4, 8], F32))
    acc = sC.enter_context(nc.sbuf_tensor("l1_acc", [128, 4, 1024], F32,
                                          side="right"))
    hT = sC.enter_context(nc.sbuf_tensor("l1_hT", [128, HC, 512], BF16,
                                         side="right"))
    gact = sC.enter_context(nc.sbuf_tensor("l1_gact", [128, 512], BF16))
    gm1 = sC.enter_context(nc.sbuf_tensor("l1_gm1", [128, 1], F32))
    gnm1 = sC.enter_context(nc.sbuf_tensor("l1_gnm1", [128, 1], F32))
    gmask = sC.enter_context(nc.sbuf_tensor("l1_gmask", [128, 8], F32))
    gl2 = sC.enter_context(nc.sbuf_tensor("l1_gl2", [128, 8], F32))
    gm2 = sC.enter_context(nc.sbuf_tensor("l1_gm2", [128, 1], F32))
    gtop = sC.enter_context(nc.sbuf_tensor("l1_gtop", [128, 8], F32))
    ge = sC.enter_context(nc.sbuf_tensor("l1_ge", [128, 8], F32))
    gsum = sC.enter_context(nc.sbuf_tensor("l1_gsum", [128, 1], F32))

    NEXP = 10
    wslot_g = [None, None]
    hT_g = [None]
    gact_g = [None]
    acc_g = [None]

    p.op("y", "dma_start", out=gw[:, :, :],
         in_=W["L1_ffn_gate_w_T"].rearrange("(c p) e -> p c e", p=128))

    def load_expert(e, sl, wtok):
        if e < 8:
            a1 = W["L1_ffn_w_gate"][e]
            a3 = W["L1_ffn_w_up"][e]
            a2 = W["L1_ffn_w_down"][e]
        else:
            hs = e - 8
            a1 = W["L1_ffn_sw_gate"][:, hs * 1408:(hs + 1) * 1408]
            a3 = W["L1_ffn_sw_up"][:, hs * 1408:(hs + 1) * 1408]
            a2 = W["L1_ffn_sw_down"][hs * 1408:(hs + 1) * 1408, :]
        p.op("g", "dma_start", out=w1[:, sl, :, :],
             in_=a1.rearrange("(c p) h -> p c h", p=128), w=(wtok,))
        p.op("g", "dma_start", out=w3[:, sl, :, :],
             in_=a3.rearrange("(c p) h -> p c h", p=128))
        return p.op("g", "dma_start", out=w2[:, sl, 0:HC, :],
                    in_=a2.rearrange("(h p) d2 -> p h d2", p=128))

    wtoks = {}

    for qd in range(NQUAD):
        wtoks[0] = load_expert(0, 0, wslot_g[0])
        for s in range(4):
            t = qd * 4 + s
            xi = (t % 2) % lc_.nx
            lc_.xn_tile_from_stats(t, xi)
            acc_init = p.op("y", "dma_start", out=acc[:, s, :],
                            in_=y_attn_d[t * 128:(t + 1) * 128, :],
                            w=(acc_g[0],))
            lc_.transpose_xn(s)
            lc_.transpose_xnf((pb[0], pb[1]))
            bk = pb[5]
            p.acquire(bk)
            for cc in range(8):
                p.op("t", "matmul", bk.t[:, 0:8],
                     lc_.xnT_f[:, cc, :], gw[:, cc, :],
                     start=(cc == 0), stop=(cc == 7),
                     w=(p.ctr["v"], p.dctr["g"], p.dctr["y"]))
            p.op("v", "reduce_max", gm1[:, :], bk.t[:, 0:8], AX.X, w=(p.ctr["t"],))
            p.op("v", "reduce_max", gnm1[:, :], bk.t[:, 0:8], AX.X, negate=True)
            p.fence("v")
            p.op("v", "tensor_scalar", gmask[:, :], bk.t[:, 0:8], gm1[:, :], None,
                 ALU.is_ge)
            p.fence("v")
            p.op("v", "scalar_tensor_tensor", gl2[:, :], gmask[:, :], NEG,
                 bk.t[:, 0:8], ALU.mult, ALU.add)
            p.fence("v")
            p.op("v", "reduce_max", gm2[:, :], gl2[:, :], AX.X)
            p.fence("v")
            bk.guard = p.op("v", "tensor_scalar", gtop[:, :], bk.t[:, 0:8],
                            gm2[:, :], None, ALU.is_ge)
            tke = p.op("s", "activation", ge[:, :], bk.t[:, 0:8], AF.Exp,
                       bias=gnm1[:, :], scale=1.0, w=(p.ctr["v"],))
            p.op("v", "tensor_mul", ge[:, :], ge[:, :], gtop[:, :], w=(tke,))
            p.fence("v")
            p.op("v", "reduce_sum", gsum[:, :], ge[:, :], AX.X)
            p.fence("v")
            p.op("v", "reciprocal", gsum[:, :], gsum[:, :])
            p.fence("v")
            p.op("v", "tensor_scalar", gts[:, s, :], ge[:, :], gsum[:, :], 2.5,
                 ALU.mult, ALU.mult, w=(acc_init,))
        xn_done = p.tok("v")

        for e in range(NEXP):
            sl = e % 2
            if e + 1 < NEXP:
                wtoks[e + 1] = load_expert(e + 1, (e + 1) % 2,
                                           wslot_g[(e + 1) % 2])
            for Hc in range(HC):
                bkG = pb[Hc % 2]
                bkU = pb[2 + Hc % 2]
                p.acquire(bkG)
                p.acquire(bkU)
                for cc in range(8):
                    p.op("t", "matmul", bkG.t[:, :],
                         w1[:, sl, cc, Hc * 128:(Hc + 1) * 128],
                         lc_.xnT[:, cc, :], start=(cc == 0), stop=(cc == 7),
                         w=(wtoks[e], xn_done))
                for cc in range(8):
                    p.op("t", "matmul", bkU.t[:, :],
                         w3[:, sl, cc, Hc * 128:(Hc + 1) * 128],
                         lc_.xnT[:, cc, :], start=(cc == 0), stop=(cc == 7))
                tks = p.op("s", "activation", gact[:, :], bkG.t[:, :], AF.Silu,
                           w=(p.ctr["t"], gact_g[0]))
                bkG.guard = tks
                tkv = p.op("v", "tensor_mul", hT[:, Hc, :], gact[:, :],
                           bkU.t[:, :], w=(tks, hT_g[0]))
                bkU.guard = tkv
                gact_g[0] = tkv
            h_done = p.tok("v")
            for sub in range(4):
                for grp in range(2):
                    bk = pb[4 + (sub * 2 + grp) % 2]
                    p.acquire(bk)
                    for Hc in range(HC):
                        p.op("t", "matmul", bk.t[:, :],
                             hT[:, Hc, sub * 128:(sub + 1) * 128],
                             w2[:, sl, Hc, grp * 512:(grp + 1) * 512],
                             start=(Hc == 0), stop=(Hc == HC - 1), w=(h_done,))
                    if e < 8:
                        bk.guard = p.op("v", "scalar_tensor_tensor",
                                        acc[:, sub, grp * 512:(grp + 1) * 512],
                                        bk.t[:, :], gts[:, sub, e:e + 1],
                                        acc[:, sub, grp * 512:(grp + 1) * 512],
                                        ALU.mult, ALU.add, w=(p.ctr["t"],))
                    else:
                        bk.guard = p.op("v", "tensor_add",
                                        acc[:, sub, grp * 512:(grp + 1) * 512],
                                        bk.t[:, :],
                                        acc[:, sub, grp * 512:(grp + 1) * 512],
                                        w=(p.ctr["t"],))
            hT_g[0] = p.tok("t")
            wslot_g[sl] = p.tok("t")
        acc_g[0] = p.op("y", "dma_start",
                        out=x_out[qd * 512:(qd + 1) * 512, :].rearrange(
                            "(s p) d -> p s d", p=128),
                        in_=acc[:, :, :], w=(p.ctr["v"],))

    sC.close()



def emit_l3(p, io, dbg=False):
    nc, es = p.nc, p.es
    sb = io["sb"]
    ident_b, ident_f = sb["ident_b"], sb["ident_f"]
    ones_b, ones_f = sb["ones_b"], sb["ones_f"]
    wmask_b = sb["wmask_b"]
    x_in, x_out, y_attn_d = io["x_in"], io["x_out"], io["y_attn_d"]
    W = io["w"]
    NH = 16

    lc_ = LayerCommon(p, sb, x_in, "l3")
    pb = lc_.pb
    ptrb = lc_.ptr

    sAB = ExitStack()
    lc_.alloc_tiles(sAB, nx=2)
    wqT = sAB.enter_context(nc.sbuf_tensor("l3_wqT", [128, 8, 1024], BF16))
    wkT = sAB.enter_context(nc.sbuf_tensor("l3_wkT", [128, 8, 1024], BF16))
    wvT = sAB.enter_context(nc.sbuf_tensor("l3_wvT", [128, 8, 1024], BF16))
    woT = sAB.enter_context(nc.sbuf_tensor("l3_woT", [128, 8, 1024], BF16))
    brows = sAB.enter_context(nc.sbuf_tensor("l3_brows", [1, 4, 1024], BF16))
    rtab = sAB.enter_context(nc.sbuf_tensor("l3_rtab", [128, 2048], F32))
    q_ro = sAB.enter_context(nc.sbuf_tensor("l3_qro", [128, 1024], BF16))
    k_ro = sAB.enter_context(nc.sbuf_tensor("l3_kro", [128, 1024], BF16))
    ropeA = sAB.enter_context(nc.sbuf_tensor("l3_ropeA", [128, 256], F32))
    ropeB = sAB.enter_context(nc.sbuf_tensor("l3_ropeB", [128, 256], F32))
    ropeC = sAB.enter_context(nc.sbuf_tensor("l3_ropeC", [128, 256], F32))
    ropeD = sAB.enter_context(nc.sbuf_tensor("l3_ropeD", [128, 256], F32))
    qT_t = sAB.enter_context(nc.sbuf_tensor("l3_qTt", [128, 8, 128], BF16))
    kT_roll = sAB.enter_context(nc.sbuf_tensor("l3_kTr", [128, 2, 8, 128], BF16))
    v_roll = sAB.enter_context(nc.sbuf_tensor("l3_vr", [128, 2, 1024], BF16))
    probs = sAB.enter_context(nc.sbuf_tensor("l3_probs", [128, 256], BF16))
    probsT = sAB.enter_context(nc.sbuf_tensor("l3_probsT", [128, 2, 128], BF16))
    mcol = sAB.enter_context(nc.sbuf_tensor("l3_mcol", [128, 1], F32))
    dencol = sAB.enter_context(nc.sbuf_tensor("l3_den", [128, 1], F32))
    reccol = sAB.enter_context(nc.sbuf_tensor("l3_rec", [128, 1], F32))
    y_t = sAB.enter_context(nc.sbuf_tensor("l3_yt", [128, 1024], F32))
    ybf = sAB.enter_context(nc.sbuf_tensor("l3_ybf", [128, 1024], BF16))
    yT = sAB.enter_context(nc.sbuf_tensor("l3_yT", [128, 8, 128], BF16))
    ya2 = sAB.enter_context(nc.sbuf_tensor("l3_ya2", [128, 1024], F32))

    wq_d = nc.dram_tensor("l3_wq_d", [1024, 1024], BF16, kind="Internal").ap()
    wk_d = nc.dram_tensor("l3_wk_d", [1024, 1024], BF16, kind="Internal").ap()
    wv_d = nc.dram_tensor("l3_wv_d", [1024, 1024], BF16, kind="Internal").ap()
    wo_d = nc.dram_tensor("l3_wo_d", [1024, 1024], BF16, kind="Internal").ap()

    g = lambda out, in_, **kw: p.op("g", "dma_start", out=out, in_=in_, **kw)
    t1 = g(wq_d, W["L3_attn_Wq"])
    t2 = g(wk_d, W["L3_attn_Wk"])
    t3 = g(wv_d, W["L3_attn_Wv"])
    t4 = g(wo_d, W["L3_attn_Wo"])
    g(brows[0:1, 0, :], W["L3_attn_bq"][None, :])
    g(brows[0:1, 1, :], W["L3_attn_bk"][None, :])
    g(brows[0:1, 2, :], W["L3_attn_bv"][None, :])
    tbr = g(brows[0:1, 3, :], W["L3_attn_bo"][None, :])
    for cc in range(8):
        p.op("y", "dma_start", out=wqT[:, cc, :],
             in_=wq_d[:, cc * 128:(cc + 1) * 128], transpose=True, w=(t1,))
        p.op("y", "dma_start", out=wkT[:, cc, :],
             in_=wk_d[:, cc * 128:(cc + 1) * 128], transpose=True, w=(t2,))
        p.op("y", "dma_start", out=wvT[:, cc, :],
             in_=wv_d[:, cc * 128:(cc + 1) * 128], transpose=True, w=(t3,))
        p.op("y", "dma_start", out=woT[:, cc, :],
             in_=wo_d[:, cc * 128:(cc + 1) * 128], transpose=True, w=(t4,))
    wT_tok = p.dtok("y")

    rtab_dma = [None]
    rtab_war = [None]
    roll_g = [None, None]
    yad_g = [None]

    def proj_rope(t, wT, brow_i, out_bf, is_q):
        for grp in range(2):
            bk = pb[4 + grp]
            p.acquire(bk)
            for cc in range(8):
                p.op("t", "matmul", bk.t[:, :],
                     lc_.xnT[:, cc, (t % 4) * 128:((t % 4) + 1) * 128],
                     wT[:, cc, grp * 512:(grp + 1) * 512],
                     start=(cc == 0), stop=False,
                     w=(p.ctr["v"], wT_tok, tbr))
            p.op("t", "matmul", bk.t[:, :], ones_b[0:1, :],
                 brows[0:1, brow_i, grp * 512:(grp + 1) * 512],
                 start=False, stop=True)
            base = 0 if is_q else 1024
            ps3 = bk.t[:, :].rearrange("p (h d) -> p h d", h=8)
            co3 = rtab[:, base + grp * 256:base + (grp + 1) * 256].rearrange(
                "p (h d) -> p h d", h=8)
            si3 = rtab[:, base + 512 + grp * 256:base + 512 + (grp + 1) * 256
                       ].rearrange("p (h d) -> p h d", h=8)
            ra3 = ropeA[:, :].rearrange("p (h d) -> p h d", h=8)
            rb3 = ropeB[:, :].rearrange("p (h d) -> p h d", h=8)
            rc3 = ropeC[:, :].rearrange("p (h d) -> p h d", h=8)
            rd3 = ropeD[:, :].rearrange("p (h d) -> p h d", h=8)
            ou3 = out_bf[:, grp * 512:(grp + 1) * 512].rearrange(
                "p (h d) -> p h d", h=8)
            p.fence("v")
            p.op("v", "tensor_mul", ra3[:, :, :], ps3[:, :, 0:32], co3[:, :, :],
                 w=(p.ctr["t"], rtab_dma[0]))
            p.op("v", "tensor_mul", rb3[:, :, :], ps3[:, :, 32:64], si3[:, :, :])
            p.op("v", "tensor_mul", rc3[:, :, :], ps3[:, :, 32:64], co3[:, :, :])
            p.op("v", "tensor_mul", rd3[:, :, :], ps3[:, :, 0:32], si3[:, :, :])
            p.fence("v")
            p.op("v", "tensor_sub", ou3[:, :, 0:32], ra3[:, :, :], rb3[:, :, :])
            bk.guard = p.op("v", "tensor_add", ou3[:, :, 32:64],
                            rc3[:, :, :], rd3[:, :, :])
            p.fence("v")
        return p.tok("v")

    for t in range(NTILE):
        par = t % 2
        lc_.ln_tile(t, t % 2)
        lc_.transpose_xn(t % 4)
        rtab_dma[0] = p.op("y", "dma_start", out=rtab[:, :],
                           in_=W["r3tab"][t * 128:(t + 1) * 128, :],
                           w=(rtab_war[0],))
        tq = proj_rope(t, wqT, 0, q_ro, True)
        tkk = proj_rope(t, wkT, 1, k_ro, False)
        rtab_war[0] = tkk
        for grp in range(2):
            bk = pb[grp]
            p.acquire(bk)
            for cc in range(8):
                p.op("t", "matmul", bk.t[:, :],
                     lc_.xnT[:, cc, (t % 4) * 128:((t % 4) + 1) * 128],
                     wvT[:, cc, grp * 512:(grp + 1) * 512],
                     start=(cc == 0), stop=False, w=(roll_g[par], wT_tok))
            p.op("t", "matmul", bk.t[:, :], ones_b[0:1, :],
                 brows[0:1, 2, grp * 512:(grp + 1) * 512],
                 start=False, stop=True)
            bk.guard = p.op("s", "copy", v_roll[:, par, grp * 512:(grp + 1) * 512],
                            bk.t[:, :], w=(p.ctr["t"],))
        p.acquire(lc_.ptr_bank)
        for dc in range(8):
            p.op("t", "transpose", ptrb[:, dc * 128:(dc + 1) * 128],
                 q_ro[:, dc * 128:(dc + 1) * 128], ident_b[:, :], w=(tq,))
        lc_.ptr_bank.guard = p.op("v", "tensor_copy", qT_t[:, :, :],
                                  ptrb[:, :].rearrange("p (c m) -> p c m", c=8),
                                  w=(p.ctr["t"],))
        p.acquire(lc_.ptr_bank)
        for dc in range(8):
            p.op("t", "transpose", ptrb[:, dc * 128:(dc + 1) * 128],
                 k_ro[:, dc * 128:(dc + 1) * 128], ident_b[:, :],
                 w=(tkk, p.ctr["v"]))
        lc_.ptr_bank.guard = p.op("v", "tensor_copy", kT_roll[:, par, :, :],
                                  ptrb[:, :].rearrange("p (c m) -> p c m", c=8),
                                  w=(p.ctr["t"], roll_g[par]))
        qk_done = p.tok("v")

        for h in range(NH):
            hb = (h % 2) * 64
            hc = h // 2
            bk = pb[2 + h % 2]
            p.acquire(bk)
            if t > 0:
                p.op("t", "matmul", bk.t[:, 0:128],
                     qT_t[hb:hb + 64, hc, :], kT_roll[hb:hb + 64, 1 - par, hc, :],
                     start=True, stop=False, w=(qk_done,))
                p.op("t", "matmul", bk.t[:, 0:128], ident_b[:, :],
                     wmask_b[:, 0, :], start=False, stop=True)
            p.op("t", "matmul", bk.t[:, 128:256],
                 qT_t[hb:hb + 64, hc, :], kT_roll[hb:hb + 64, par, hc, :],
                 start=True, stop=False, w=(qk_done,))
            p.op("t", "matmul", bk.t[:, 128:256], ident_b[:, :],
                 wmask_b[:, 1, :], start=False, stop=True)
            co = 0 if t > 0 else 128
            p.op("v", "reduce_max", mcol[:, :], bk.t[:, co:256], AX.X,
                 negate=True, w=(p.ctr["t"],))
            p.fence("v")
            tke = p.op("s", "activation", probs[:, co:256], bk.t[:, co:256],
                       AF.Exp, bias=mcol[:, :], scale=1.0,
                       accum_out=dencol[:, :], w=(p.ctr["v"],))
            bk.guard = tke
            p.op("v", "reciprocal", reccol[:, :], dencol[:, :], w=(tke,))
            p.acquire(lc_.ptr_bank)
            nblk = 2 if t > 0 else 1
            for j in range(nblk):
                p.op("t", "transpose", ptrb[:, j * 128:(j + 1) * 128],
                     probs[:, co + j * 128:co + (j + 1) * 128], ident_b[:, :],
                     w=(tke,))
            lc_.ptr_bank.guard = p.op(
                "v", "tensor_copy", probsT[:, 0:nblk, :],
                ptrb[:, 0:nblk * 128].rearrange("p (c m) -> p c m", m=128),
                w=(p.ctr["t"],))
            pT_done = p.tok("v")
            bk2 = pb[h % 2]
            p.acquire(bk2)
            if t > 0:
                p.op("t", "matmul", bk2.t[:, 0:64], probsT[:, 0, :],
                     v_roll[:, 1 - par, h * 64:(h + 1) * 64],
                     start=True, stop=False, w=(pT_done,))
                p.op("t", "matmul", bk2.t[:, 0:64], probsT[:, 1, :],
                     v_roll[:, par, h * 64:(h + 1) * 64],
                     start=False, stop=True)
            else:
                p.op("t", "matmul", bk2.t[:, 0:64], probsT[:, 0, :],
                     v_roll[:, par, h * 64:(h + 1) * 64],
                     start=True, stop=True, w=(pT_done,))
            bk2.guard = p.op("v", "tensor_scalar", y_t[:, h * 64:(h + 1) * 64],
                             bk2.t[:, 0:64], reccol[:, :], None, ALU.mult,
                             w=(p.ctr["t"],))
        tyc = p.op("s", "copy", ybf[:, :], y_t[:, :], w=(p.ctr["v"],))
        p.acquire(lc_.ptr_bank)
        for dc in range(8):
            p.op("t", "transpose", ptrb[:, dc * 128:(dc + 1) * 128],
                 ybf[:, dc * 128:(dc + 1) * 128], ident_b[:, :], w=(tyc,))
        lc_.ptr_bank.guard = p.op("v", "tensor_copy", yT[:, :, :],
                                  ptrb[:, :].rearrange("p (c m) -> p c m", c=8),
                                  w=(p.ctr["t"],))
        yT_done = p.tok("v")
        for grp in range(2):
            bk = pb[4 + grp]
            p.acquire(bk)
            for cc in range(8):
                p.op("t", "matmul", bk.t[:, :], yT[:, cc, :],
                     woT[:, cc, grp * 512:(grp + 1) * 512],
                     start=(cc == 0), stop=False, w=(yT_done,))
            p.op("t", "matmul", bk.t[:, :], ones_b[0:1, :],
                 brows[0:1, 3, grp * 512:(grp + 1) * 512],
                 start=False, stop=True)
            bk.guard = p.op("v", "tensor_copy", ya2[:, grp * 512:(grp + 1) * 512],
                            bk.t[:, :], w=(p.ctr["t"], yad_g[0]))
        yad_g[0] = p.op("y", "dma_start", out=y_attn_d[t * 128:(t + 1) * 128, :],
                        in_=ya2[:, :], w=(p.ctr["v"],))
        roll_g[par] = p.tok("t")

    p.barrier()
    sAB.close()

    mean2 = es.enter_context(nc.sbuf_tensor("l3_mean2", [128, NTILE], F32))
    rstd2 = es.enter_context(nc.sbuf_tensor("l3_rstd2", [128, NTILE], F32))
    gbc = es.enter_context(nc.sbuf_tensor("l3_gbc", [128, 1024], F32))
    bbc = es.enter_context(nc.sbuf_tensor("l3_bbc", [128, 1024], F32))
    grow = es.enter_context(nc.sbuf_tensor("l3_grow", [1, 2, 1024], F32))

    tg = p.op("y", "dma_start", out=grow[0:1, 0, :], in_=W["L3_ffn_ln_g"][None, :])
    tb = p.op("y", "dma_start", out=grow[0:1, 1, :], in_=W["L3_ffn_ln_b"][None, :])
    for grp in range(2):
        bk = pb[grp]
        p.acquire(bk)
        p.op("t", "matmul", bk.t[:, :], ones_f[0:1, :],
             grow[0:1, 0, grp * 512:(grp + 1) * 512], start=True, stop=True,
             w=(tg, tb))
        bk.guard = p.op("v", "tensor_copy", gbc[:, grp * 512:(grp + 1) * 512],
                        bk.t[:, :], w=(p.ctr["t"],))
        bk2 = pb[2 + grp]
        p.acquire(bk2)
        p.op("t", "matmul", bk2.t[:, :], ones_f[0:1, :],
             grow[0:1, 1, grp * 512:(grp + 1) * 512], start=True, stop=True)
        bk2.guard = p.op("v", "tensor_copy", bbc[:, grp * 512:(grp + 1) * 512],
                         bk2.t[:, :], w=(p.ctr["t"],))

    def xn2_tile(t, xi, first):
        tk = p.op("y", "dma_start", out=lc_.x_sb[:, xi, :],
                  in_=x_in[t * 128:(t + 1) * 128, :], w=(lc_.xg[xi],))
        tk = p.op("v", "tensor_scalar", lc_.xn_f[:, :], lc_.x_sb[:, xi, :],
                  lc_.mean_all[:, t:t + 1], lc_.rstd_all[:, t:t + 1],
                  ALU.subtract, ALU.mult, w=(tk, p.ctr["t"], p.ctr["s"]))
        lc_.xg[xi] = tk
        if first:
            tks = p.op("s", "activation", lc_.sq_scr[:, :], lc_.xn_f[:, :],
                       AF.Square, accum_out=lc_.cols[:, 1:2], w=(tk,))
            p.op("v", "reduce_sum", lc_.cols[:, 0:1], lc_.xn_f[:, :], AX.X)
            p.fence("v")
            p.op("v", "tensor_scalar", lc_.mv[:, 0:1], lc_.cols[:, 0:1],
                 1.0 / D, None, ALU.mult)
            p.fence("v")
            p.op("v", "tensor_mul", lc_.cols[:, 2:3], lc_.mv[:, 0:1],
                 lc_.mv[:, 0:1])
            p.op("v", "tensor_scalar", lc_.cols[:, 3:4], lc_.cols[:, 1:2],
                 1.0 / D, None, ALU.mult, w=(tks,))
            p.fence("v")
            p.op("v", "tensor_sub", lc_.cols[:, 3:4], lc_.cols[:, 3:4],
                 lc_.cols[:, 2:3])
            p.fence("v")
            p.op("s", "activation", lc_.sd_col[:, :], lc_.cols[:, 3:4], AF.Sqrt,
                 bias=sb["eps_col"][:, :], w=(p.ctr["v"],))
            p.op("v", "reciprocal", lc_.rstd_col[:, :], lc_.sd_col[:, :],
                 w=(p.ctr["s"],))
            p.fence("v")
            p.op("v", "tensor_copy", mean2[:, t:t + 1], lc_.mv[:, 0:1])
            p.op("v", "tensor_copy", rstd2[:, t:t + 1], lc_.rstd_col[:, :])
        p.op("v", "tensor_scalar", lc_.xn_f[:, :], lc_.xn_f[:, :],
             mean2[:, t:t + 1], rstd2[:, t:t + 1], ALU.subtract, ALU.mult)
        p.fence("v")
        p.op("v", "tensor_mul", lc_.xn_f[:, :], lc_.xn_f[:, :], gbc[:, :])
        p.fence("v")
        tk = p.op("v", "tensor_add", lc_.xn_f[:, :], lc_.xn_f[:, :], bbc[:, :])
        tk = p.op("s", "copy", lc_.xn_tok[:, :], lc_.xn_f[:, :], w=(tk,))
        return tk

    abel_d = nc.dram_tensor("l3_abel_d", [1024, T], BF16, kind="Internal").ap()
    nona_d = nc.dram_tensor("l3_nona_d", [1024, T], BF16, kind="Internal").ap()
    moe_d = nc.dram_tensor("l3_moe_d", [T, 1024], F32, kind="Internal").ap()

    sC1 = ExitStack()
    lc_.alloc_tiles(sC1, nx=1, with_f32=True)
    aw1 = sC1.enter_context(nc.sbuf_tensor("l3_aw1", [128, 2, 8, 2048], BF16))
    aw2 = sC1.enter_context(nc.sbuf_tensor("l3_aw2", [128, 2, 16, 1024], BF16))
    ab1pc = sC1.enter_context(nc.sbuf_tensor("l3_ab1pc", [128, 32], F32))
    ab2pc = sC1.enter_context(nc.sbuf_tensor("l3_ab2pc", [128, 8], F32))
    hT = sC1.enter_context(nc.sbuf_tensor("l3_hT", [128, 16, 512], BF16,
                                          side="right"))
    ab_st = sC1.enter_context(nc.sbuf_tensor("l3_abst", [128, 8, 512], BF16,
                                             side="right"))

    p.op("y", "dma_start", out=ab1pc[:, :], in_=W["L3_ffn_ab1_pc"])
    tpc = p.op("y", "dma_start", out=ab2pc[:, :], in_=W["L3_ffn_ab2_pc"])

    def load_ahalf(hf, sl, wtok):
        p.op("g", "dma_start", out=aw1[:, sl, :, :],
             in_=W["L3_ffn_aw1"][:, hf * 2048:(hf + 1) * 2048].rearrange(
                 "(c p) h -> p c h", p=128), w=(wtok,))
        return p.op("g", "dma_start", out=aw2[:, sl, :, :],
                    in_=W["L3_ffn_aw2"][hf * 2048:(hf + 1) * 2048, :].rearrange(
                        "(h p) d2 -> p h d2", p=128))

    aslot_g = [None, None]
    hT_g = [None]
    ab_g = [None]
    atoks = {}
    for qd in range(NQUAD):
        atoks[0] = load_ahalf(0, 0, aslot_g[0])
        atoks[1] = load_ahalf(1, 1, aslot_g[1])
        for s in range(4):
            t = qd * 4 + s
            xn2_tile(t, 0, first=True)
            lc_.transpose_xn(s)
        xn_done = p.tok("v")
        for hf in range(2):
            sl = hf
            for Hc in range(16):
                bk = pb[Hc % 2]
                p.acquire(bk)
                for cc in range(8):
                    p.op("t", "matmul", bk.t[:, :],
                         aw1[:, sl, cc, Hc * 128:(Hc + 1) * 128],
                         lc_.xnT[:, cc, :], start=(cc == 0), stop=(cc == 7),
                         w=(atoks[hf], xn_done, tpc))
                bk.guard = p.op("s", "activation", hT[:, Hc, :], bk.t[:, :],
                                AF.Gelu,
                                bias=ab1pc[:, hf * 16 + Hc:hf * 16 + Hc + 1],
                                w=(p.ctr["t"], hT_g[0]))
            h_done = p.tok("s")
            for eb in range(8):
                bk = pb[2 + eb % 2]
                p.acquire(bk)
                for Hc in range(16):
                    p.op("t", "matmul", bk.t[:, :],
                         aw2[:, sl, Hc, eb * 128:(eb + 1) * 128],
                         hT[:, Hc, :], start=(Hc == 0), stop=(Hc == 15),
                         w=(h_done,))
                if hf == 0:
                    bk.guard = p.op("v", "tensor_scalar", ab_st[:, eb, :],
                                    bk.t[:, :], ab2pc[:, eb:eb + 1], None,
                                    ALU.add, w=(p.ctr["t"], ab_g[0]))
                else:
                    bk.guard = p.op("v", "tensor_add", ab_st[:, eb, :],
                                    ab_st[:, eb, :], bk.t[:, :],
                                    w=(p.ctr["t"], ab_g[0]))
            hT_g[0] = p.tok("t")
            aslot_g[sl] = p.tok("t")
        last = None
        for eb in range(8):
            last = p.op("y", "dma_start",
                        out=abel_d[eb * 128:(eb + 1) * 128,
                                   qd * 512:(qd + 1) * 512],
                        in_=ab_st[:, eb, :], w=(p.ctr["v"], p.ctr["s"]))
        ab_g[0] = last
    p.barrier()
    sC1.close()

    sC2 = ExitStack()
    lc_.alloc_tiles(sC2, nx=1, with_f32=True)
    gwf = sC2.enter_context(nc.sbuf_tensor("l3_gwf", [128, 8, 6], F32))
    gbrow = sC2.enter_context(nc.sbuf_tensor("l3_gbrow", [1, 6], F32))
    ew1 = sC2.enter_context(nc.sbuf_tensor("l3_ew1", [128, 2, 8, 1024], BF16))
    ew2 = sC2.enter_context(nc.sbuf_tensor("l3_ew2", [128, 2, 8, 1024], BF16))
    eb1pc = sC2.enter_context(nc.sbuf_tensor("l3_eb1pc", [128, 6, 16], F32))
    eb2row = sC2.enter_context(nc.sbuf_tensor("l3_eb2r", [1, 2, 1024], BF16))
    gts = sC2.enter_context(nc.sbuf_tensor("l3_gts", [128, 4, 6], F32))
    acc = sC2.enter_context(nc.sbuf_tensor("l3_acc", [128, 4, 1024], F32,
                                           side="right"))
    hT2 = sC2.enter_context(nc.sbuf_tensor("l3_hT2", [128, 8, 512], BF16,
                                           side="right"))
    gm1 = sC2.enter_context(nc.sbuf_tensor("l3_gm1", [128, 1], F32))
    gnm1 = sC2.enter_context(nc.sbuf_tensor("l3_gnm1", [128, 1], F32))
    gmask = sC2.enter_context(nc.sbuf_tensor("l3_gmask", [128, 6], F32))
    gl2 = sC2.enter_context(nc.sbuf_tensor("l3_gl2", [128, 6], F32))
    gm2 = sC2.enter_context(nc.sbuf_tensor("l3_gm2", [128, 1], F32))
    gtop = sC2.enter_context(nc.sbuf_tensor("l3_gtop", [128, 6], F32))
    ge = sC2.enter_context(nc.sbuf_tensor("l3_ge", [128, 6], F32))
    gsum = sC2.enter_context(nc.sbuf_tensor("l3_gsum", [128, 1], F32))

    p.op("y", "dma_start", out=gwf[:, :, :],
         in_=W["L3_ffn_gate_w"].rearrange("(c p) e -> p c e", p=128))
    p.op("y", "dma_start", out=gbrow[0:1, :], in_=W["L3_ffn_gate_b"][None, :])
    p.op("y", "dma_start", out=eb1pc[:, :, :],
         in_=W["L3_ffn_eb1_pc"].rearrange("e p h -> p e h"))

    def load_ehalf(e, hf, sl, wtok):
        p.op("g", "dma_start", out=ew1[:, sl, :, :],
             in_=W["L3_ffn_ew1"][e][:, hf * 1024:(hf + 1) * 1024].rearrange(
                 "(c p) h -> p c h", p=128), w=(wtok,))
        tk = p.op("g", "dma_start", out=ew2[:, sl, :, :],
                  in_=W["L3_ffn_ew2"][e][hf * 1024:(hf + 1) * 1024, :].rearrange(
                      "(h p) d2 -> p h d2", p=128))
        if hf == 0:
            tk = p.op("g", "dma_start", out=eb2row[0:1, sl, :],
                      in_=W["L3_ffn_eb2"][e][None, :])
        return tk

    eslot_g = [None, None]
    hT2_g = [None]
    acc_g = [None]
    etoks = {}
    HE = 12

    for qd in range(NQUAD):
        etoks[0] = load_ehalf(0, 0, 0, eslot_g[0])
        for s in range(4):
            t = qd * 4 + s
            xn2_tile(t, 0, first=False)
            lc_.transpose_xn(s)
            lc_.transpose_xnf((pb[0], pb[1]))
            p.op("v", "memset", acc[:, s, :], 0.0, w=(acc_g[0],))
            bk = pb[5]
            p.acquire(bk)
            for cc in range(8):
                p.op("t", "matmul", bk.t[:, 0:6], lc_.xnT_f[:, cc, :],
                     gwf[:, cc, :], start=(cc == 0), stop=False,
                     w=(p.ctr["v"], p.dctr["y"]))
            p.op("t", "matmul", bk.t[:, 0:6], ones_f[0:1, :], gbrow[0:1, :],
                 start=False, stop=True)
            p.op("v", "reduce_max", gm1[:, :], bk.t[:, 0:6], AX.X, w=(p.ctr["t"],))
            p.op("v", "reduce_max", gnm1[:, :], bk.t[:, 0:6], AX.X, negate=True)
            p.fence("v")
            p.op("v", "tensor_scalar", gmask[:, :], bk.t[:, 0:6], gm1[:, :], None,
                 ALU.is_ge)
            p.fence("v")
            p.op("v", "scalar_tensor_tensor", gl2[:, :], gmask[:, :], NEG,
                 bk.t[:, 0:6], ALU.mult, ALU.add)
            p.fence("v")
            p.op("v", "reduce_max", gm2[:, :], gl2[:, :], AX.X)
            p.fence("v")
            bk.guard = p.op("v", "tensor_scalar", gtop[:, :], bk.t[:, 0:6],
                            gm2[:, :], None, ALU.is_ge)
            tke = p.op("s", "activation", ge[:, :], bk.t[:, 0:6], AF.Exp,
                       bias=gnm1[:, :], scale=1.0, w=(p.ctr["v"],))
            p.op("v", "tensor_mul", ge[:, :], ge[:, :], gtop[:, :], w=(tke,))
            p.fence("v")
            p.op("v", "reduce_sum", gsum[:, :], ge[:, :], AX.X)
            p.fence("v")
            p.op("v", "reciprocal", gsum[:, :], gsum[:, :])
            p.fence("v")
            p.op("v", "tensor_scalar", gts[:, s, :], ge[:, :], gsum[:, :], None,
                 ALU.mult)
        xn_done = p.tok("v")

        for he in range(HE):
            e, hf = he // 2, he % 2
            sl = he % 2
            if he + 1 < HE:
                etoks[he + 1] = load_ehalf((he + 1) // 2, (he + 1) % 2,
                                           (he + 1) % 2, eslot_g[(he + 1) % 2])
            for Hc in range(8):
                bk = pb[Hc % 2]
                p.acquire(bk)
                for cc in range(8):
                    p.op("t", "matmul", bk.t[:, :],
                         ew1[:, sl, cc, Hc * 128:(Hc + 1) * 128],
                         lc_.xnT[:, cc, :], start=(cc == 0), stop=(cc == 7),
                         w=(etoks[he], xn_done))
                bk.guard = p.op("s", "activation", hT2[:, Hc, :], bk.t[:, :],
                                AF.Gelu,
                                bias=eb1pc[:, e, hf * 8 + Hc:hf * 8 + Hc + 1],
                                w=(p.ctr["t"], hT2_g[0]))
            h_done = p.tok("s")
            for sub in range(4):
                for grp in range(2):
                    bk = pb[2 + (sub * 2 + grp) % 2]
                    p.acquire(bk)
                    for Hc in range(8):
                        p.op("t", "matmul", bk.t[:, :],
                             hT2[:, Hc, sub * 128:(sub + 1) * 128],
                             ew2[:, sl, Hc, grp * 512:(grp + 1) * 512],
                             start=(Hc == 0),
                             stop=(Hc == 7 and hf == 1), w=(h_done,))
                    if hf == 0:
                        p.op("t", "matmul", bk.t[:, :], ones_b[0:1, :],
                             eb2row[0:1, sl, grp * 512:(grp + 1) * 512],
                             start=False, stop=True)
                    bk.guard = p.op("v", "scalar_tensor_tensor",
                                    acc[:, sub, grp * 512:(grp + 1) * 512],
                                    bk.t[:, :], gts[:, sub, e:e + 1],
                                    acc[:, sub, grp * 512:(grp + 1) * 512],
                                    ALU.mult, ALU.add, w=(p.ctr["t"],))
            hT2_g[0] = p.tok("t")
            eslot_g[sl] = p.tok("t")
        acc_g[0] = p.op("y", "dma_start",
                        out=moe_d[qd * 512:(qd + 1) * 512, :].rearrange(
                            "(s p) d -> p s d", p=128),
                        in_=acc[:, :, :], w=(p.ctr["v"],))
    p.barrier()
    sC2.close()

    sC3 = ExitStack()
    AT = sC3.enter_context(nc.sbuf_tensor("l3_AT", [128, 8, 1024], BF16))
    Bb = sC3.enter_context(nc.sbuf_tensor("l3_B", [128, 2, 8, 1024], BF16))
    Rb = sC3.enter_context(nc.sbuf_tensor("l3_R", [128, 2, 8, 1024], BF16,
                                          side="right"))
    moeT = sC3.enter_context(nc.sbuf_tensor("l3_moeT", [128, 8, 512], BF16))
    mt_f = sC3.enter_context(nc.sbuf_tensor("l3_mtf", [128, 1024], F32))
    mt_b = sC3.enter_context(nc.sbuf_tensor("l3_mtb", [128, 1024], BF16))
    na_st = sC3.enter_context(nc.sbuf_tensor("l3_nast", [128, 8, 512], BF16))
    gen_d = nc.dram_tensor("l3_gen_d", [1024, 1024], BF16, kind="Internal").ap()

    def mat_transpose(src, slot, dst, wtok):
        for cb in range(8):
            p.acquire(lc_.ptr_bank)
            for rb in range(8):
                p.op("t", "transpose", ptrb[:, rb * 128:(rb + 1) * 128],
                     src[:, slot, rb, cb * 128:(cb + 1) * 128], ident_b[:, :],
                     w=wtok)
            lc_.ptr_bank.guard = p.op("v", "tensor_copy", dst[:, cb, :],
                                      ptrb[:, :], w=(p.ctr["t"],))
        return p.tok("v")

    def matmul_1024(lhsT_at, rhs_at, out_at, scale=None, diag_at=None,
                    wtok=()):
        w0 = tuple(wtok) + (p.ctr["s"], p.ctr["v"])
        for eb in range(8):
            for grp in range(2):
                bk = pb[(eb * 2 + grp) % 4]
                p.acquire(bk)
                for cc in range(8):
                    p.op("t", "matmul", bk.t[:, :], lhsT_at(cc, eb),
                         rhs_at(cc, grp), start=(cc == 0), stop=(cc == 7),
                         w=w0)
                if scale is not None:
                    bk.guard = p.op("s", "activation", out_at(eb, grp),
                                    bk.t[:, :], AF.Copy, scale=scale,
                                    w=(p.ctr["t"],))
                else:
                    bk.guard = p.op("s", "copy", out_at(eb, grp), bk.t[:, :],
                                    w=(p.ctr["t"],))
        if diag_at is not None:
            for eb in range(8):
                p.op("v", "tensor_add", diag_at(eb), diag_at(eb),
                     ident_b[:, :], w=(p.ctr["s"],))

    Rcur = 0
    for gi in range(4):
        tgd = p.op("g", "dma_start", out=gen_d, in_=W["L3_ffn_gens"][gi],
                   w=(p.ctr["t"],))
        for cc in range(8):
            p.op("y", "dma_start", out=AT[:, cc, :],
                 in_=gen_d[:, cc * 128:(cc + 1) * 128], transpose=True,
                 w=(tgd,))
        at_tok = p.dtok("y")
        tkm = p.op("v", "memset", Bb[:, 0, :, :], 0.0, w=(p.ctr["t"], at_tok))
        for eb in range(8):
            p.op("v", "tensor_copy", Bb[:, 0, eb, eb * 128:(eb + 1) * 128],
                 ident_b[:, :])
        p.fence("v")
        cur = 0
        for k in range(6, 0, -1):
            src, dst = cur, 1 - cur
            matmul_1024(
                lambda cc, eb: AT[:, cc, eb * 128:(eb + 1) * 128],
                lambda cc, grp, _s=src: Bb[:, _s, cc, grp * 512:(grp + 1) * 512],
                lambda eb, grp, _d=dst: Bb[:, _d, eb, grp * 512:(grp + 1) * 512],
                scale=1.0 / (4.0 * k),
                diag_at=lambda eb, _d=dst: Bb[:, _d, eb, eb * 128:(eb + 1) * 128],
                wtok=(at_tok,))
            cur = 1 - cur
        for sq in range(2):
            mat_transpose(Bb, cur, AT, (p.ctr["s"], p.ctr["v"]))
            src, dst = cur, 1 - cur
            matmul_1024(
                lambda cc, eb: AT[:, cc, eb * 128:(eb + 1) * 128],
                lambda cc, grp, _s=src: Bb[:, _s, cc, grp * 512:(grp + 1) * 512],
                lambda eb, grp, _d=dst: Bb[:, _d, eb, grp * 512:(grp + 1) * 512])
            cur = 1 - cur
        if gi == 0:
            for eb in range(8):
                p.op("v", "tensor_copy", Rb[:, Rcur, eb, :], Bb[:, cur, eb, :],
                     w=(p.ctr["s"],))
            p.fence("v")
        else:
            mat_transpose(Rb, Rcur, AT, (p.ctr["s"], p.ctr["v"]))
            src, dst = cur, 1 - Rcur
            matmul_1024(
                lambda cc, eb: AT[:, cc, eb * 128:(eb + 1) * 128],
                lambda cc, grp, _s=src: Bb[:, _s, cc, grp * 512:(grp + 1) * 512],
                lambda eb, grp, _d=dst: Rb[:, _d, eb, grp * 512:(grp + 1) * 512])
            Rcur = 1 - Rcur
    p.barrier()

    mt_g = [None]
    na_g = [None]
    for qd in range(NQUAD):
        for s in range(4):
            t = qd * 4 + s
            tkm = p.op("y", "dma_start", out=mt_f[:, :],
                       in_=moe_d[t * 128:(t + 1) * 128, :], w=(mt_g[0],))
            tkc = p.op("s", "copy", mt_b[:, :], mt_f[:, :], w=(tkm, p.ctr["t"]))
            mt_g[0] = tkc
            p.acquire(lc_.ptr_bank)
            for dc in range(8):
                p.op("t", "transpose", ptrb[:, dc * 128:(dc + 1) * 128],
                     mt_b[:, dc * 128:(dc + 1) * 128], ident_b[:, :], w=(tkc,))
            lc_.ptr_bank.guard = p.op(
                "v", "tensor_copy", moeT[:, :, s * 128:(s + 1) * 128],
                ptrb[:, :].rearrange("p (c m) -> p c m", c=8),
                w=(p.ctr["t"], p.ctr["s"]))
        moeT_done = p.tok("v")
        for eb in range(8):
            bk = pb[4 + eb % 2]
            p.acquire(bk)
            for cc in range(8):
                p.op("t", "matmul", bk.t[:, :],
                     Rb[:, Rcur, cc, eb * 128:(eb + 1) * 128],
                     moeT[:, cc, :], start=(cc == 0), stop=(cc == 7),
                     w=(moeT_done,))
            bk.guard = p.op("s", "copy", na_st[:, eb, :], bk.t[:, :],
                            w=(p.ctr["t"], na_g[0]))
        na_g[0] = p.op("y", "dma_start",
                       out=nona_d[:, qd * 512:(qd + 1) * 512].rearrange(
                           "(c p) m -> p c m", p=128),
                       in_=na_st[:, :, :], w=(p.ctr["s"],))
    p.barrier()
    sC3.close()

    sC4 = ExitStack()
    lc_.alloc_tiles(sC4, nx=1, with_f32=True)
    gcw = sC4.enter_context(nc.sbuf_tensor("l3_gcw", [128, 16], BF16))
    gcb = sC4.enter_context(nc.sbuf_tensor("l3_gcb", [1, 1], BF16))
    abT = sC4.enter_context(nc.sbuf_tensor("l3_abT", [128, 8, 128], BF16))
    naT = sC4.enter_context(nc.sbuf_tensor("l3_naT", [128, 8, 128], BF16))
    ab_tok = sC4.enter_context(nc.sbuf_tensor("l3_abtok", [128, 1024], BF16))
    na_tok = sC4.enter_context(nc.sbuf_tensor("l3_natok", [128, 1024], BF16))
    gcol = sC4.enter_context(nc.sbuf_tensor("l3_gcol", [128, 1], F32))
    dif = sC4.enter_context(nc.sbuf_tensor("l3_dif", [128, 1024], BF16))
    mixd = sC4.enter_context(nc.sbuf_tensor("l3_mixd", [128, 1024], F32))
    ya_sb = sC4.enter_context(nc.sbuf_tensor("l3_ya4", [128, 1024], F32))
    outt = sC4.enter_context(nc.sbuf_tensor("l3_out", [128, 1024], F32))

    p.op("g", "dma_start", out=gcw[:, :], in_=W["L3_ffn_gc_w_pc"])
    p.op("g", "dma_start", out=gcb[0:1, :], in_=W["L3_ffn_gc_b"][None, :])

    aT_g = [None]
    ya_g = [None]
    out_g = [None]
    for t in range(NTILE):
        tka = p.op("y", "dma_start", out=abT[:, :, :],
                   in_=abel_d[:, t * 128:(t + 1) * 128].rearrange(
                       "(c p) m -> p c m", p=128), w=(aT_g[0],))
        tkn = p.op("y", "dma_start", out=naT[:, :, :],
                   in_=nona_d[:, t * 128:(t + 1) * 128].rearrange(
                       "(c p) m -> p c m", p=128))
        tky = p.op("y", "dma_start", out=ya_sb[:, :],
                   in_=y_attn_d[t * 128:(t + 1) * 128, :], w=(ya_g[0],))
        tkx = p.op("y", "dma_start", out=lc_.x_sb[:, 0, :],
                   in_=x_in[t * 128:(t + 1) * 128, :], w=(lc_.xg[0],))
        tkxn = p.op("v", "tensor_scalar", lc_.xn_f[:, :], lc_.x_sb[:, 0, :],
                    lc_.mean_all[:, t:t + 1], lc_.rstd_all[:, t:t + 1],
                    ALU.subtract, ALU.mult, w=(tkx, p.ctr["t"], p.ctr["s"]))
        lc_.xg[0] = tkxn
        bk = pb[5]
        p.acquire(bk)
        for cc in range(8):
            p.op("t", "matmul", bk.t[:, 0:1], abT[:, cc, :], gcw[:, cc:cc + 1],
                 start=(cc == 0), stop=False, w=(tka, tkn, p.dctr["g"]))
        for cc in range(8):
            p.op("t", "matmul", bk.t[:, 0:1], naT[:, cc, :],
                 gcw[:, 8 + cc:9 + cc], start=False, stop=False)
        p.op("t", "matmul", bk.t[:, 0:1], ones_b[0:1, :], gcb[0:1, :],
             start=False, stop=True)
        tkg = p.op("s", "activation", gcol[:, :], bk.t[:, 0:1], AF.Sigmoid,
                   w=(p.ctr["t"],))
        bk.guard = tkg
        p.acquire(lc_.ptr_bank)
        for dc in range(8):
            p.op("t", "transpose", ptrb[:, dc * 128:(dc + 1) * 128],
                 abT[:, dc, :], ident_b[:, :], w=(tka, p.ctr["v"]))
        lc_.ptr_bank.guard = p.op("v", "tensor_copy", ab_tok[:, :], ptrb[:, :],
                                  w=(p.ctr["t"],))
        p.acquire(lc_.ptr_bank)
        for dc in range(8):
            p.op("t", "transpose", ptrb[:, dc * 128:(dc + 1) * 128],
                 naT[:, dc, :], ident_b[:, :], w=(tkn, p.ctr["v"]))
        lc_.ptr_bank.guard = p.op("v", "tensor_copy", na_tok[:, :], ptrb[:, :],
                                  w=(p.ctr["t"],))
        aT_g[0] = p.tok("t")
        p.fence("v")
        p.op("v", "tensor_sub", dif[:, :], na_tok[:, :], ab_tok[:, :])
        p.fence("v")
        p.op("v", "scalar_tensor_tensor", mixd[:, :], dif[:, :], gcol[:, :],
             ab_tok[:, :], ALU.mult, ALU.add, w=(tkg,))
        p.fence("v")
        p.op("v", "tensor_add", outt[:, :], lc_.x_sb[:, 0, :], ya_sb[:, :],
             w=(tky, out_g[0]))
        ya_g[0] = p.tok("v")
        p.fence("v")
        p.op("v", "tensor_add", outt[:, :], outt[:, :], lc_.xn_f[:, :])
        p.fence("v")
        p.op("v", "tensor_add", outt[:, :], outt[:, :], mixd[:, :])
        out_g[0] = p.op("y", "dma_start", out=x_out[t * 128:(t + 1) * 128, :],
                        in_=outt[:, :], w=(p.ctr["v"],))
    p.barrier()
    sC4.close()



W_SHAPES = {
    "L0_attn_W_dq": (512, 1024), "L0_attn_W_uq": (1024, 512),
    "L0_attn_W_dkv": (512, 1024), "L0_attn_W_uk": (1024, 512),
    "L0_attn_W_uv": (1024, 512), "L0_attn_W_o": (1024, 1024),
    "L0_ffn_router_w": (1024, 8), "L0_ffn_router_b": (8,),
    "L0_ffn_routed_w1": (8, 1024, 512), "L0_ffn_routed_w3": (8, 1024, 512),
    "L0_ffn_routed_w2": (8, 512, 1024),
    "L0_ffn_shared_w1": (2, 1024, 512), "L0_ffn_shared_w3": (2, 1024, 512),
    "L0_ffn_shared_w2": (2, 512, 1024),
}

W_SHAPES.update({
    "L2_attn_W_dq": (128, 1024), "L2_attn_W_uq": (1024, 128),
    "L2_attn_W_dkv": (128, 1024), "L2_attn_W_uk": (1024, 128),
    "L2_attn_W_uv": (1024, 128), "L2_attn_W_o": (1024, 1024),
    "L2_ffn_W_r": (1024, 1024), "L2_ffn_W_k": (1024, 1024),
    "L2_ffn_W_v": (1024, 1024),
    "L2_ffn_mu_r_pc": (128, 8), "L2_ffn_mu_k_pc": (128, 8),
    "L1_attn_Wq": (1024, 1024), "L1_attn_Wqr": (1024, 512),
    "L1_attn_Wkr": (1024, 32), "L1_attn_Wkvd": (1024, 256),
    "L1_attn_Wku": (256, 1024), "L1_attn_Wvu": (256, 1024),
    "L1_attn_Wo": (1024, 1024), "L1_attn_lobo_log": (16,),
    "L1_ffn_gate_w_T": (1024, 8),
    "L1_ffn_w_gate": (8, 1024, 1408), "L1_ffn_w_up": (8, 1024, 1408),
    "L1_ffn_w_down": (8, 1408, 1024),
    "L1_ffn_sw_gate": (1024, 2816), "L1_ffn_sw_up": (1024, 2816),
    "L1_ffn_sw_down": (2816, 1024),
    "r1tab": (4096, 544),
    "L3_attn_Wq": (1024, 1024), "L3_attn_bq": (1024,),
    "L3_attn_Wk": (1024, 1024), "L3_attn_bk": (1024,),
    "L3_attn_Wv": (1024, 1024), "L3_attn_bv": (1024,),
    "L3_attn_Wo": (1024, 1024), "L3_attn_bo": (1024,),
    "L3_ffn_ln_g": (1024,), "L3_ffn_ln_b": (1024,),
    "L3_ffn_aw1": (1024, 4096), "L3_ffn_ab1_pc": (128, 32),
    "L3_ffn_aw2": (4096, 1024), "L3_ffn_ab2_pc": (128, 8),
    "L3_ffn_gate_w": (1024, 6), "L3_ffn_gate_b": (6,),
    "L3_ffn_ew1": (6, 1024, 2048), "L3_ffn_eb1_pc": (6, 128, 16),
    "L3_ffn_ew2": (6, 2048, 1024), "L3_ffn_eb2": (6, 1024),
    "L3_ffn_gens": (4, 1024, 1024),
    "L3_ffn_gc_w": (2048, 1), "L3_ffn_gc_w_pc": (128, 16), "L3_ffn_gc_b": (1,),
    "r3tab": (4096, 2048),
})

LAYER_WNAMES = {
    0: [n for n in W_SHAPES if n.startswith("L0")],
    1: [n for n in W_SHAPES if n.startswith("L1")] + ["r1tab"],
    2: [n for n in W_SHAPES if n.startswith("L2")],
    3: [n for n in W_SHAPES if n.startswith("L3")] + ["r3tab"],
}

LAYER_EMIT = {0: emit_l0, 1: emit_l1, 2: emit_l2, 3: emit_l3}


def _make_r1tab():
    half = 16
    inv = 1.0 / 10000.0 ** (np.arange(half, dtype=np.float32) / half)
    ang = np.arange(T, dtype=np.float32)[:, None] * inv[None, :]
    cos1 = np.cos(ang).astype(np.float32)
    sin1 = np.sin(ang).astype(np.float32)
    sc = np.float32(1.0 / np.sqrt(96.0))
    tab = np.zeros((T, 544), dtype=np.float32)
    tab[:, 0:256] = np.tile(cos1 * sc, (1, 16))
    tab[:, 256:512] = np.tile(sin1 * sc, (1, 16))
    tab[:, 512:528] = cos1
    tab[:, 528:544] = sin1
    return tab


def _make_r3tab():
    half = 32
    inv = 1.0 / 10000.0 ** (np.arange(half, dtype=np.float32) / half)
    ang = np.arange(T, dtype=np.float32)[:, None] * inv[None, :]
    cos3 = np.cos(ang).astype(np.float32)
    sin3 = np.sin(ang).astype(np.float32)
    sc = np.float32(1.0 / 8.0)
    tab = np.zeros((T, 2048), dtype=np.float32)
    tab[:, 0:512] = np.tile(cos3 * sc, (1, 16))
    tab[:, 512:1024] = np.tile(sin3 * sc, (1, 16))
    tab[:, 1024:1536] = np.tile(cos3, (1, 16))
    tab[:, 1536:2048] = np.tile(sin3, (1, 16))
    return tab


TABLE_GEN = {"r1tab": _make_r1tab, "r3tab": _make_r3tab}


def build(layers=(0,), want_taps=(), dump_inter=False):
    nc = bass.Bass("TRN2", target_bir_lowering=False, debug=False)
    es = ExitStack()
    p = Prog(nc, es)
    p.want_taps = set(want_taps)

    C = host_constants()
    x0 = nc.dram_tensor("x0", [T, D], F32, kind="ExternalInput").ap()
    xout = nc.dram_tensor("xout", [T, D], F32, kind="ExternalOutput").ap()
    xs = [nc.dram_tensor("x_scr1", [T, D], F32, kind="Internal").ap(),
          nc.dram_tensor("x_scr2", [T, D], F32, kind="Internal").ap()]
    y_attn_d = nc.dram_tensor("y_attn_d", [T, D], F32, kind="Internal").ap()

    cin = {name: nc.dram_tensor(f"c_{name}", list(arr.shape), F32,
                                kind="ExternalInput").ap()
           for name, arr in C.items()}
    wdram = {}
    for li in layers:
        for name in LAYER_WNAMES[li]:
            wdram[name] = nc.dram_tensor(name, list(W_SHAPES[name]), F32,
                                         kind="ExternalInput").ap()

    sb = {}
    sb["ident_f"] = es.enter_context(nc.sbuf_tensor("k_identf", [128, 128], F32))
    sb["ident_b"] = es.enter_context(nc.sbuf_tensor("k_identb", [128, 128], BF16))
    sb["ones_b"] = es.enter_context(nc.sbuf_tensor("k_onesb", [128, 128], BF16))
    sb["ones_f"] = es.enter_context(nc.sbuf_tensor("k_onesf", [128, 128], F32))
    sb["cmask"] = es.enter_context(nc.sbuf_tensor("k_cmask", [128, 4, 512], F32))
    sb["eps_col"] = es.enter_context(nc.sbuf_tensor("k_eps", [128, 1], F32))
    sb["wmask"] = es.enter_context(nc.sbuf_tensor("k_wmask", [128, 2, 128], F32))
    sb["wmask_b"] = es.enter_context(nc.sbuf_tensor("k_wmaskb", [128, 2, 128], BF16))
    _flv = es.enter_context(nc.sbuf_tensor("k_flv", [1, 8], F32))
    _fls = es.enter_context(nc.sbuf_tensor("k_fls", [1, 8], F32))
    p.flt = {"v": _flv[0:1, 0:4], "s": _fls[0:1, 0:4]}
    p.op("y", "dma_start", out=sb["ident_f"][:, :], in_=cin["ident_f"])
    p.op("g", "dma_start", out=sb["ident_b"][:, :], in_=cin["ident_b"])
    p.op("g", "dma_start", out=sb["ones_b"][:, :], in_=cin["ones_f"])
    p.op("y", "dma_start", out=sb["ones_f"][:, :], in_=cin["ones_f"])
    p.op("y", "dma_start", out=sb["cmask"][:, :, :], in_=cin["cmask"])
    p.op("y", "dma_start", out=sb["eps_col"][:, :], in_=cin["eps_col"])
    p.op("y", "dma_start", out=sb["wmask"][:, :, :], in_=cin["wmask"])
    p.op("g", "dma_start", out=sb["wmask_b"][:, :, :], in_=cin["wmask"])
    p.barrier()

    import os as _os2
    for i, li in enumerate(layers):
        io = {
            "x_in": x0 if i == 0 else xs[(i - 1) % 2],
            "x_out": xout if i == len(layers) - 1 else xs[i % 2],
            "y_attn_d": y_attn_d,
            "w": wdram,
            "sb": sb,
        }
        if li == 1 and _os2.environ.get("L1_DBGP"):
            io["dbg_probs"] = nc.dram_tensor("dbg_probs", [16, 4, 128, 512],
                                             BF16, kind="ExternalOutput").ap()
            io["dbg_yaug"] = nc.dram_tensor("dbg_yaug", [16, 65, 512], F32,
                                            kind="ExternalOutput").ap()
        with ExitStack() as les:
            old_es = p.es
            p.es = les
            LAYER_EMIT[li](p, io, dbg=bool(p.want_taps))
            p.es = old_es
        if dump_inter and i < len(layers) - 1:
            dmp = nc.dram_tensor(f"xl{i}", [T, D], F32, kind="ExternalOutput").ap()
            with ExitStack() as les:
                bt = les.enter_context(nc.sbuf_tensor(f"dmp{i}", [128, D], F32))
                last = None
                for tt in range(NTILE):
                    tk = p.op("y", "dma_start", out=bt[:, :],
                              in_=io["x_out"][tt * 128:(tt + 1) * 128, :],
                              w=(last,))
                    last = p.op("y", "dma_start",
                                out=dmp[tt * 128:(tt + 1) * 128, :],
                                in_=bt[:, :], w=(tk,))
                p.waitonly("y", last)
                p.barrier()

    p.waitonly("y", p.dtok("y"), p.dtok("g"))
    p.finalize()
    return nc, C, es, p


_tab_cache = {}



def make_in_maps(C, layers, inputs):
    x = np.ascontiguousarray(inputs["L0_attn_x"])
    in_maps = []
    for c in range(NCORES):
        m = {"x0": np.ascontiguousarray(x[c])}
        for name, arr in C.items():
            m[f"c_{name}"] = arr
        for li in layers:
            for name in LAYER_WNAMES[li]:
                if name in inputs:
                    m[name] = np.ascontiguousarray(inputs[name])
                elif name in TABLE_GEN:
                    if name not in _tab_cache:
                        _tab_cache[name] = TABLE_GEN[name]()
                    m[name] = _tab_cache[name]
                elif name == "L3_ffn_eb1_pc":
                    m[name] = np.ascontiguousarray(
                        inputs["L3_ffn_eb1"].reshape(6, 16, 128).transpose(0, 2, 1))
                elif name.endswith("_pc"):
                    base = inputs[name[:-3]]
                    m[name] = np.ascontiguousarray(base.reshape(-1, 128).T)
                elif name.endswith("_T"):
                    m[name] = np.ascontiguousarray(inputs[name[:-2]].T)
                else:
                    raise KeyError(name)
        in_maps.append(m)
    return in_maps


def kernel(**inputs):
    layers = (0, 1, 2, 3)
    nc, C, es, p = build(layers)
    in_maps = make_in_maps(C, layers, inputs)
    res = run_bass_kernel_spmd(nc, in_maps, core_ids=list(range(NCORES)))
    es.close()
    return np.stack([r["xout"] for r in res.results])
